# version 15
# baseline (speedup 1.0000x reference)
import numpy as np
from contextlib import ExitStack

import concourse.bass as bass
import concourse.mybir as mybir
from concourse.bass_utils import run_bass_kernel_spmd

F32 = mybir.dt.float32
BF16 = mybir.dt.bfloat16
AF = mybir.ActivationFunctionType
ALU = mybir.AluOpType

D_MODEL = 1024
D_HID = 4096
T_FULL = 2048
N_CORES = 8
CHUNK = 512
N_CHUNKS = T_FULL // CHUNK
TSUB = 128
N_TSUB = CHUNK // TSUB
N_HEADS = 16
D_HEAD = 64
N_PAIRS = 8
EPS = 1e-5

IC_D = D_MODEL // 128
IC_H = D_HID // 128

N_WSLOTS = 4
WSLOT_W = 4096



class Op:
    __slots__ = ("thunk", "waits", "inc", "tag")

    def __init__(self, thunk, waits, inc, tag):
        self.thunk = thunk
        self.waits = waits
        self.inc = inc
        self.tag = tag


class Prog:

    ENGINES = ("te", "sc", "ve", "po", "sy")

    def __init__(self):
        self.ops = {e: [] for e in self.ENGINES}
        self.sem_count = {}
        self.eng_sem = {"te": "s_te", "sc": "s_sc", "ve": "s_ve",
                        "po": "s_po", "sy": "s_sy"}
        for s in self.eng_sem.values():
            self.sem_count[s] = 0
        self.buf = {}
        self.knowledge = {e: {} for e in self.ENGINES}

    def _b(self, key):
        if key not in self.buf:
            self.buf[key] = {"w": None, "r": []}
        return self.buf[key]

    def emit(self, eng, thunk, reads=(), writes=(), dma_sem=None, tag=""):
        waits = {}

        def dep(token):
            if token is None:
                return
            s, v = token
            if waits.get(s, 0) < v:
                waits[s] = v

        for k in reads:
            dep(self._b(k)["w"])
        for k in writes:
            b = self._b(k)
            dep(b["w"])
            for r in b["r"]:
                dep(r)
        kn = self.knowledge[eng]
        wl = []
        for s, v in waits.items():
            if kn.get(s, 0) >= v:
                continue
            wl.append((s, v))
            kn[s] = v
        if dma_sem is not None:
            self.sem_count.setdefault(dma_sem, 0)
            self.sem_count[dma_sem] += 16
            token = (dma_sem, self.sem_count[dma_sem])
            inc = (dma_sem, 16)
        else:
            s = self.eng_sem[eng]
            self.sem_count[s] += 1
            token = (s, self.sem_count[s])
            inc = (s, 1)
        for k in reads:
            self._b(k)["r"].append(token)
        for k in writes:
            b = self._b(k)
            b["w"] = token
            b["r"] = []
        self.ops[eng].append(Op(thunk, wl, inc, tag))
        return token

    def barrier_token(self, key):
        return self._b(key)["w"]

    def n_instructions(self):
        return sum(len(v) for v in self.ops.values())



def build_program(n_chunks=N_CHUNKS, phases=("f1", "att", "f2")):
    nc = bass.Bass("TRN2", target_bir_lowering=False)

    dt_in = {}

    def din(name, shape):
        dt_in[name] = nc.dram_tensor(name, shape, F32, kind="ExternalInput").ap()
        return dt_in[name]

    xT = din("xT", [D_MODEL, T_FULL])
    w1 = din("w1", [IC_H, 128, IC_D, 128])
    w2 = din("w2", [IC_H, 128, IC_D, 128])
    h1 = din("h1", [IC_H, 128, IC_H, 128])
    h2 = din("h2", [IC_H, 128, IC_H, 128])
    w3 = din("w3", [IC_D, 128, IC_H, 128])
    wq = din("wq", [IC_D, 128, IC_D, 128])
    wk = din("wk", [IC_D, 128, IC_D, 128])
    wv = din("wv", [IC_D, 128, IC_D, 128])
    wo = din("wo", [IC_D, 128, IC_D, 128])
    s1 = din("s1", [128, IC_H])
    s2 = din("s2", [128, IC_H])
    sh1 = din("sh1", [128, IC_H])
    sh2 = din("sh2", [128, IC_H])
    s3 = din("s3", [128, IC_D])
    qg = din("qg", [128, 1])
    qb = din("qb", [128, 1])
    kg = din("kg", [128, 1])
    kb = din("kb", [128, 1])
    identc = din("identc", [128, 128])
    maskc = din("maskc", [128, 128])
    mextc = din("mextc", [128, CHUNK])
    onesc = din("onesc", [128, 1])
    onesrow = din("onesrow", [1, 128])
    hmask = din("hmask", [128, 2])
    hmaskT = din("hmaskT", [2, 128])
    outT = nc.dram_tensor("outT", [D_MODEL, T_FULL], F32, kind="ExternalOutput").ap()

    scr = {}
    for nm, t in (("w1", w1), ("w2", w2), ("h1", h1), ("h2", h2), ("w3", w3),
                  ("wq", wq), ("wk", wk), ("wv", wv), ("wo", wo)):
        scr[nm] = nc.dram_tensor("scr_" + nm, list(t.tensor.shape), BF16,
                                 kind="Internal").ap()

    P = Prog()

    with ExitStack() as stack:
        E = stack.enter_context

        xs = [E(nc.sbuf_tensor(f"xs{i}", [128, IC_D * CHUNK], F32)) for i in range(2)]
        u = E(nc.sbuf_tensor("u", [128, IC_D * CHUNK], BF16))
        hA = E(nc.sbuf_tensor("hA", [128, IC_H * CHUNK], BF16))
        hB = E(nc.sbuf_tensor("hB", [128, IC_H * CHUNK], BF16))
        wsl = [E(nc.sbuf_tensor(f"wsl{i}", [128, WSLOT_W], BF16)) for i in range(N_WSLOTS)]
        sq = [E(nc.sbuf_tensor(f"sq{i}", [128, CHUNK], BF16)) for i in range(2)]
        cqs = [E(nc.sbuf_tensor(f"cq{i}", [128, CHUNK], BF16)) for i in range(2)]
        tqs = [E(nc.sbuf_tensor(f"tq{i}", [128, CHUNK], BF16)) for i in range(2)]
        srow_mu = [E(nc.sbuf_tensor(f"srmu{i}", [2, CHUNK], BF16)) for i in range(2)]
        srow_rs = [E(nc.sbuf_tensor(f"srrs{i}", [2, CHUNK], BF16)) for i in range(2)]
        Sf = E(nc.sbuf_tensor("Sf", [128, N_PAIRS * D_HEAD], F32))
        SbfE = E(nc.sbuf_tensor("SbfE", [128, N_PAIRS * D_HEAD], BF16))
        SbfO = E(nc.sbuf_tensor("SbfO", [128, N_PAIRS * D_HEAD], BF16))
        asbh = [E(nc.sbuf_tensor(f"asbh{i}", [128, 2 * 1280], BF16))
                for i in range(2)]
        ktmw = [E(nc.sbuf_tensor(f"ktmw{i}", [128, CHUNK], BF16)) for i in range(2)]
        strow = E(nc.sbuf_tensor("strow", [2, CHUNK], BF16))
        strow2 = strow
        consts = E(nc.sbuf_tensor("consts", [128, 128 + 128 + 1 + 2], BF16))
        mext = E(nc.sbuf_tensor("mext", [128, CHUNK], BF16))
        musb = E(nc.sbuf_tensor("musb", [128, CHUNK], BF16))
        rsb = E(nc.sbuf_tensor("rsb", [128, CHUNK], BF16))
        onesr = E(nc.sbuf_tensor("onesr", [1, 128], BF16))
        hmT = E(nc.sbuf_tensor("hmT", [2, 128], BF16))
        scl = E(nc.sbuf_tensor("scl", [128, 4 * IC_H + 2 * IC_D + 4], F32))
        epsv = E(nc.sbuf_tensor("epsv", [128, 1], F32))
        onesf = E(nc.sbuf_tensor("onesf", [128, 1], F32))
        cen = hB[:, 0:IC_D * CHUNK]
        xbf = cen
        OVL = E(nc.sbuf_tensor("OVL", [128, 4 * IC_D * CHUNK], BF16))
        qraw = OVL[:, 0:4096]
        kraw = OVL[:, 4096:8192]
        vtm = OVL[:, 8192:12288]
        ofm = OVL[:, 12288:16384]
        qln, kln = qraw, kraw
        uf = E(nc.sbuf_tensor("uf", [128, IC_D * CHUNK], BF16))
        hG = [hA, hB]

        identb = consts[:, 0:128]
        maskb = consts[:, 128:256]
        onescol = consts[:, 256:257]
        hmaskb = consts[:, 257:259]
        s1v = scl[:, 0:IC_H]
        s2v = scl[:, IC_H:2 * IC_H]
        sh1v = scl[:, 2 * IC_H:3 * IC_H]
        sh2v = scl[:, 3 * IC_H:4 * IC_H]
        s3v = scl[:, 4 * IC_H:4 * IC_H + IC_D]
        s3hv = scl[:, 4 * IC_H + IC_D:4 * IC_H + 2 * IC_D]
        qgv = scl[:, 4 * IC_H + 2 * IC_D + 0:4 * IC_H + 2 * IC_D + 1]
        qbv = scl[:, 4 * IC_H + 2 * IC_D + 1:4 * IC_H + 2 * IC_D + 2]
        kgv = scl[:, 4 * IC_H + 2 * IC_D + 2:4 * IC_H + 2 * IC_D + 3]
        kbv = scl[:, 4 * IC_H + 2 * IC_D + 3:4 * IC_H + 2 * IC_D + 4]

        pb = [E(nc.psum_tensor(f"pb{i}", [128, 512], F32)) for i in range(8)]
        pb_bf = [b[:].tensor.bitcast(BF16).ap() for b in pb]

        sem_names = (["s_te", "s_sc", "s_ve", "s_po", "s_sy"]
                     + [f"wsl{i}" for i in range(N_WSLOTS)]
                     + [f"wslc{i}" for i in range(N_WSLOTS)]
                     + [f"wst{i}" for i in range(N_WSLOTS)]
                     + ["xl0", "xl1", "st0", "st1", "cst"])
        sems = {n: E(nc.semaphore(n)) for n in sem_names}


        def TE(thunk, reads, writes, tag=""):
            return P.emit("te", thunk, reads, writes, tag=tag)

        def SC(thunk, reads, writes, tag=""):
            return P.emit("sc", thunk, reads, writes, tag=tag)

        def VE(thunk, reads, writes, tag=""):
            return P.emit("ve", thunk, reads, writes, tag=tag)

        def PO(thunk, reads, writes, tag=""):
            return P.emit("po", thunk, reads, writes, tag=tag)

        def DMA(q, thunk, reads, writes, sem, tag=""):
            return P.emit(q, thunk, reads, writes, dma_sem=sem, tag=tag)

        DMA("po", lambda g: g.dma_start(out=consts[:, 0:128], in_=identc),
            [], ["consts", "cstall"], "cst", "ident")
        DMA("po", lambda g: g.dma_start(out=mext[:], in_=mextc),
            [], ["mext", "cstall"], "cst", "mext")
        DMA("po", lambda g: g.dma_start(out=consts[:, 256:257], in_=onesc),
            [], ["consts", "cstall"], "cst", "ones")
        DMA("po", lambda g: g.dma_start(out=onesr[:], in_=onesrow),
            [], ["onesr", "cstall"], "cst", "onesrow")
        DMA("po", lambda g: g.dma_start(out=hmT[:], in_=hmaskT),
            [], ["hmT", "cstall"], "cst", "hmT")
        DMA("po", lambda g: g.dma_start(out=consts[:, 257:259], in_=hmask),
            [], ["consts", "cstall"], "cst", "hmask")
        DMA("po", lambda g: g.dma_start(out=scl[:, 0:IC_H], in_=s1),
            [], ["scl", "cstall"], "cst", "s1")
        DMA("po", lambda g: g.dma_start(out=scl[:, IC_H:2 * IC_H], in_=s2),
            [], ["scl", "cstall"], "cst", "s2")
        DMA("po", lambda g: g.dma_start(out=scl[:, 2 * IC_H:3 * IC_H], in_=sh1),
            [], ["scl", "cstall"], "cst", "sh1")
        DMA("po", lambda g: g.dma_start(out=scl[:, 3 * IC_H:4 * IC_H], in_=sh2),
            [], ["scl", "cstall"], "cst", "sh2")
        DMA("po", lambda g: g.dma_start(out=scl[:, 4 * IC_H:4 * IC_H + IC_D], in_=s3),
            [], ["scl", "cstall"], "cst", "s3")
        base = 4 * IC_H + 2 * IC_D
        for i, t in enumerate((qg, qb, kg, kb)):
            DMA("po", lambda g, i=i, t=t: g.dma_start(
                out=scl[:, base + i:base + i + 1], in_=t),
                [], ["scl", "cstall"], "cst", "qkgb")
        VE(lambda v: v.memset(epsv[:], EPS), [], ["epsv"], "eps")
        VE(lambda v: v.memset(onesf[:], 1.0), [], ["onesf"], "onesf")
        VE(lambda v: v.tensor_scalar_mul(s3hv, s3v, 0.5), ["scl"], ["scl"], "s3h")
        VE(lambda v: v.memset(Sf[:], 0.0), [], ["Sf"], "S0")
        VE(lambda v: v.memset(SbfE[:], 0.0), [], ["SbfE"], "SbfE0")
        VE(lambda v: v.memset(SbfO[:], 0.0), [], ["SbfO"], "SbfO0")

        ring_state = {"next": 0}
        fp32_src = {"w1": w1, "w2": w2, "h1": h1, "h2": h2, "w3": w3,
                    "wq": wq, "wk": wk, "wv": wv, "wo": wo}
        scr_built = set()

        def load_slabs(mat, obs, tag):
            n_kc = {"w1": IC_D, "w2": IC_D, "h1": IC_H, "h2": IC_H,
                    "w3": IC_H, "wq": IC_D, "wk": IC_D, "wv": IC_D,
                    "wo": IC_D}[mat]
            slot = ring_state["next"]
            ring_state["next"] = (slot + 1) % N_WSLOTS
            w_per_slab = n_kc * 128
            assert w_per_slab * len(obs) <= WSLOT_W
            gkey = f"scrg_{mat}.{obs[0]}"
            sbuf_view = wsl[slot][:, 0:w_per_slab * len(obs)].rearrange(
                "p (s ko) -> p s ko", s=len(obs))
            if (mat, obs[0]) not in scr_built:
                scr_built.add((mat, obs[0]))
                src_ap = fp32_src[mat]

                def thunk(g, src_ap=src_ap, obs=obs, sbuf_view=sbuf_view):
                    o0 = obs[0]
                    return g.dma_start(
                        out=sbuf_view,
                        in_=src_ap[o0:o0 + len(obs)].rearrange(
                            "s p k o -> p s (k o)"))
                DMA("po", thunk, [], [f"wslot.{slot}"], f"wslc{slot}",
                    f"cld_{tag}")
                sc_ap = scr[mat]

                def sthunk(g, sc_ap=sc_ap, obs=obs, sbuf_view=sbuf_view):
                    o0 = obs[0]
                    return g.dma_start(
                        out=sc_ap[o0:o0 + len(obs)].rearrange(
                            "s p k o -> p s (k o)"),
                        in_=sbuf_view)
                DMA("sy", sthunk, [f"wslot.{slot}"], [gkey],
                    f"wst{slot}", f"sst_{tag}")
            else:
                sc_ap = scr[mat]

                def thunk(g, sc_ap=sc_ap, obs=obs, sbuf_view=sbuf_view):
                    o0 = obs[0]
                    return g.dma_start(
                        out=sbuf_view,
                        in_=sc_ap[o0:o0 + len(obs)].rearrange(
                            "s p k o -> p s (k o)"))
                DMA("sy", thunk, [gkey], [f"wslot.{slot}"], f"wsl{slot}",
                    f"ld_{tag}")
            return slot, n_kc

        def slab_tile(slot, slab_i, n_kc, kc):
            off = (slab_i * n_kc + kc) * 128
            return wsl[slot][:, off:off + 128]

        B_DS = 4
        psum_rr = {"att": 0, "ffn": 0}
        POOLS = {"att": (0, 1, 2, 3), "ffn": (5, 6, 7)}

        def next_bank(pool="ffn", exclude=()):
            ring = POOLS[pool]
            for _ in range(len(ring)):
                i = ring[psum_rr[pool] % len(ring)]
                psum_rr[pool] += 1
                if i not in exclude:
                    return i
            raise RuntimeError

        def ovl_keys(ob):
            if ob < 8:
                return [f"qraw.{ob}"]
            if ob < 16:
                return [f"kraw.{ob - 8}"]
            if ob < 24:
                c0 = (ob - 16) * 512
                p0 = (c0 % 1024) // 128
                return [f"vtm.{c0 // 1024}.{p}" for p in range(p0, p0 + 4)]
            return [f"ofm.{ob - 24}"]
        ALL_OVL = [k for ob in range(IC_H) for k in ovl_keys(ob)]

        def hh_view(ob):
            return OVL[:, ob * CHUNK:(ob + 1) * CHUNK]


        def drain(g):
            for _ in g:
                pass

        def gen_layernorm(xslot, xkey, ubuf, ukp, tag, use_pool=False):
            xv = xs[xslot]
            b_st = next_bank("ffn")
            def mm_mu(te):
                for ic in range(IC_D):
                    inst = te.matmul(pb[b_st][0:1, 0:CHUNK], onesf[:],
                                     xv[:, ic * CHUNK:(ic + 1) * CHUNK],
                                     start=(ic == 0), stop=(ic == IC_D - 1))
                return inst
            TE(mm_mu, [f"{xkey}.{ic}" for ic in range(IC_D)] + ["onesf"],
               [f"ps.{b_st}"], f"{tag}_musum")
            yield
            SC(lambda s: s.activation(strow[0:1, :], pb[b_st][0:1, 0:CHUNK],
                                      AF.Copy, scale=1.0 / D_MODEL),
               [f"ps.{b_st}"], ["strow"], f"{tag}_murow")
            b_mu = next_bank("ffn")
            TE(lambda te: te.matmul(pb[b_mu][:, 0:CHUNK], onesr[:], strow[0:1, :],
                                    start=True, stop=True),
               ["strow", "onesr"], [f"ps.{b_mu}"], f"{tag}_mubc")
            SC(lambda s: s.copy(musb[:], pb[b_mu][:, 0:CHUNK]),
               [f"ps.{b_mu}"], ["musb"], f"{tag}_musb")
            yield
            b_sq = next_bank("ffn")
            mub = musb[:].rearrange("p (a t) -> p a t", a=1).broadcast_to(
                [128, 4, CHUNK])
            for hf, eng in ((0, VE), (1, PO)):
                eng(lambda e, hf=hf: e.tensor_sub(
                    cen[:, hf * 4 * CHUNK:(hf + 1) * 4 * CHUNK].rearrange(
                        "p (i t) -> p i t", t=CHUNK),
                    xv[:, hf * 4 * CHUNK:(hf + 1) * 4 * CHUNK].rearrange(
                        "p (i t) -> p i t", t=CHUNK), mub),
                    [f"{xkey}.{ic}" for ic in range(hf * 4, hf * 4 + 4)]
                    + ["musb"],
                    [f"cen.h{hf}", "xbf"]
                    + [f"hg1.{ic}" for ic in range(hf * 4, hf * 4 + 4)],
                    f"{tag}_cen{hf}")
            for ic in range(IC_D):
                eng2 = PO if ic % 2 == 0 else VE
                eng2(lambda e, ic=ic: e.tensor_mul(
                    sq[ic % 2][:], cen[:, ic * CHUNK:(ic + 1) * CHUNK],
                    cen[:, ic * CHUNK:(ic + 1) * CHUNK]),
                    [f"cen.h{ic // 4}", "xbf"], [f"sq.{ic % 2}"], f"{tag}_sq{ic}")
                if ic % 2 == 1:
                    def mm_sq(te, ic=ic):
                        for j in (ic - 1, ic):
                            inst = te.matmul(pb[b_sq][0:1, 0:CHUNK], onescol,
                                             sq[j % 2][:],
                                             start=(j == 0), stop=(j == IC_D - 1))
                        return inst
                    TE(mm_sq, ["sq.0", "sq.1", "consts"],
                       [f"ps.{b_sq}"], f"{tag}_sqs{ic}")
            yield
            SC(lambda s: s.activation(strow2[0:1, :], pb[b_sq][0:1, 0:CHUNK],
                                      AF.Ln, scale=1.0 / D_MODEL, bias=epsv[0:1, :]),
               [f"ps.{b_sq}", "epsv"], ["strow"], f"{tag}_ln")
            SC(lambda s: s.activation(strow2[0:1, :], strow2[0:1, :],
                                      AF.Exp, scale=-0.5),
               ["strow"], ["strow"], f"{tag}_exp")
            b_rs = next_bank("ffn")
            TE(lambda te: te.matmul(pb[b_rs][:, 0:CHUNK], onesr[:],
                                    strow2[0:1, :], start=True, stop=True),
               ["strow", "onesr"], [f"ps.{b_rs}"], f"{tag}_rsbc")
            SC(lambda s: s.copy(rsb[:], pb[b_rs][:, 0:CHUNK]),
               [f"ps.{b_rs}"], ["rsb"], f"{tag}_rsb")
            yield
            rsbb = rsb[:].rearrange("p (a t) -> p a t", a=1).broadcast_to(
                [128, 4, CHUNK])
            for hf, eng in ((0, VE), (1, PO)):
                eng(lambda e, hf=hf, ubuf=ubuf: e.tensor_mul(
                    ubuf[:, hf * 4 * CHUNK:(hf + 1) * 4 * CHUNK].rearrange(
                        "p (i t) -> p i t", t=CHUNK),
                    cen[:, hf * 4 * CHUNK:(hf + 1) * 4 * CHUNK].rearrange(
                        "p (i t) -> p i t", t=CHUNK), rsbb),
                    [f"cen.h{hf}", "xbf", "rsb"],
                    [f"{ukp}.{ic}" for ic in range(hf * 4, hf * 4 + 4)],
                    f"{tag}_u{hf}")

        def gen_gate(gpar, ubuf, ukp, tag):
            G = hG[gpar]
            gk = f"hg{gpar}"
            for obg in range(IC_H // 4):
                sl1, _ = load_slabs("w1", list(range(obg * 4, obg * 4 + 4)), f"{tag}w1g{obg}")
                sl2, _ = load_slabs("w2", list(range(obg * 4, obg * 4 + 4)), f"{tag}w2g{obg}")
                for j in range(4):
                    ob = obg * 4 + j
                    ba = next_bank("ffn")
                    def mm_a(te, sl1=sl1, j=j, ba=ba, ubuf=ubuf):
                        for kc in range(IC_D):
                            inst = te.matmul(pb[ba][:, 0:CHUNK],
                                             slab_tile(sl1, j, IC_D, kc),
                                             ubuf[:, kc * CHUNK:(kc + 1) * CHUNK],
                                             start=(kc == 0), stop=(kc == IC_D - 1))
                        return inst
                    TE(mm_a, [f"wslot.{sl1}"] + [f"{ukp}.{i}" for i in range(IC_D)],
                       [f"ps.{ba}"], f"{tag}_a{ob}")
                    SC(lambda s, ba=ba, ob=ob, G=G: s.activation(
                        G[:, ob * CHUNK:(ob + 1) * CHUNK], pb[ba][:, 0:CHUNK],
                        AF.Silu, scale=s1v[:, ob:ob + 1]),
                       [f"ps.{ba}", "scl"], [f"{gk}.{ob}"], f"{tag}_silu{ob}")
                    bb = next_bank("ffn")
                    def mm_b(te, sl2=sl2, j=j, bb=bb, ubuf=ubuf):
                        for kc in range(IC_D):
                            inst = te.matmul(pb[bb][:, 0:CHUNK],
                                             slab_tile(sl2, j, IC_D, kc),
                                             ubuf[:, kc * CHUNK:(kc + 1) * CHUNK],
                                             start=(kc == 0), stop=(kc == IC_D - 1))
                        return inst
                    TE(mm_b, [f"wslot.{sl2}"] + [f"{ukp}.{i}" for i in range(IC_D)],
                       [f"ps.{bb}"], f"{tag}_b{ob}")
                    VE(lambda v, bb=bb, ob=ob, G=G: v.scalar_tensor_tensor(
                        G[:, ob * CHUNK:(ob + 1) * CHUNK], pb[bb][:, 0:CHUNK],
                        s2v[:, ob:ob + 1], G[:, ob * CHUNK:(ob + 1) * CHUNK],
                        ALU.mult, ALU.mult),
                       [f"ps.{bb}", "scl", f"{gk}.{ob}"], [f"{gk}.{ob}"],
                       f"{tag}_gate{ob}")
                    yield

        def emit_hw3(gpar, xslot, xkey, tag):
            xv = xs[xslot]
            G = hG[gpar]
            gk = f"hg{gpar}"
            for ob in range(IC_H):
                sl, _ = load_slabs("h1", [ob], f"{tag}h1s{ob}")
                bk = next_bank("ffn")
                def mm_h1(te, sl=sl, bk=bk, G=G):
                    for kc in range(IC_H):
                        inst = te.matmul(pb[bk][:, 0:CHUNK],
                                         slab_tile(sl, 0, IC_H, kc),
                                         G[:, kc * CHUNK:(kc + 1) * CHUNK],
                                         start=(kc == 0), stop=(kc == IC_H - 1))
                    return inst
                TE(mm_h1, [f"wslot.{sl}"] + [f"{gk}.{i}" for i in range(IC_H)],
                   [f"ps.{bk}"], f"{tag}_h1m{ob}")
                SC(lambda s, bk=bk, ob=ob: s.activation(
                    hh_view(ob), pb[bk][:, 0:CHUNK],
                    AF.Gelu, scale=sh1v[:, ob:ob + 1]),
                   [f"ps.{bk}", "scl"], ovl_keys(ob), f"{tag}_h1g{ob}")
            for ob in range(IC_H):
                sl, _ = load_slabs("h2", [ob], f"{tag}h2s{ob}")
                bk = next_bank("ffn")
                def mm_h2(te, sl=sl, bk=bk):
                    for kc in range(IC_H):
                        inst = te.matmul(pb[bk][:, 0:CHUNK],
                                         slab_tile(sl, 0, IC_H, kc),
                                         hh_view(kc),
                                         start=(kc == 0), stop=(kc == IC_H - 1))
                    return inst
                TE(mm_h2, [f"wslot.{sl}"] + ALL_OVL,
                   [f"ps.{bk}"], f"{tag}_h2m{ob}")
                SC(lambda s, bk=bk, ob=ob, G=G: s.activation(
                    G[:, ob * CHUNK:(ob + 1) * CHUNK], pb[bk][:, 0:CHUNK],
                    AF.Gelu, scale=sh2v[:, ob:ob + 1]),
                   [f"ps.{bk}", "scl"], [f"{gk}.{ob}"], f"{tag}_h2g{ob}")
            for ob in range(IC_D):
                sl, _ = load_slabs("w3", [ob], f"{tag}w3s{ob}")
                bk = next_bank("ffn")
                def mm_w3(te, sl=sl, bk=bk, G=G):
                    for kc in range(IC_H):
                        inst = te.matmul(pb[bk][:, 0:CHUNK],
                                         slab_tile(sl, 0, IC_H, kc),
                                         G[:, kc * CHUNK:(kc + 1) * CHUNK],
                                         start=(kc == 0), stop=(kc == IC_H - 1))
                    return inst
                TE(mm_w3, [f"wslot.{sl}"] + [f"{gk}.{i}" for i in range(IC_H)],
                   [f"ps.{bk}"], f"{tag}_w3m{ob}")
                VE(lambda v, bk=bk, ob=ob: v.scalar_tensor_tensor(
                    xv[:, ob * CHUNK:(ob + 1) * CHUNK], pb[bk][:, 0:CHUNK],
                    s3hv[:, ob:ob + 1], xv[:, ob * CHUNK:(ob + 1) * CHUNK],
                    ALU.mult, ALU.add),
                   [f"ps.{bk}", "scl", f"{xkey}.{ob}"], [f"{xkey}.{ob}"],
                   f"{tag}_res{ob}")

        def gen_qk_headln(src_t, srckey, gv, bv, si, tag):
            cq_, tq_ = cqs[si], tqs[si]
            smu, srs = srow_mu[si], srow_rs[si]
            ck, tk, muk, rsk = f"cq{si}", f"tq{si}", f"smu{si}", f"srs{si}"
            for ob in range(IC_D):
                svw = src_t[:, ob * CHUNK:(ob + 1) * CHUNK]
                b_st = next_bank("att")
                TE(lambda te, svw=svw, b_st=b_st: te.matmul(
                    pb[b_st][0:2, 0:CHUNK], hmaskb, svw, start=True, stop=True),
                   [f"{srckey}.{ob}", "consts"], [f"ps.{b_st}"], f"{tag}_mu{ob}")
                SC(lambda s, b_st=b_st, smu=smu: s.activation(
                    smu[:, :], pb[b_st][0:2, 0:CHUNK], AF.Copy, scale=1.0 / D_HEAD),
                   [f"ps.{b_st}"], [muk], f"{tag}_mur{ob}")
                b_mu = next_bank("att")
                TE(lambda te, b_mu=b_mu, smu=smu: te.matmul(
                    pb[b_mu][:, 0:CHUNK], hmT[:], smu[:, :], start=True, stop=True),
                   [muk, "hmT"], [f"ps.{b_mu}"], f"{tag}_mub{ob}")
                VE(lambda v, svw=svw, b_mu=b_mu, cq_=cq_: v.tensor_sub(
                    cq_[:], svw, pb[b_mu][:, 0:CHUNK]),
                   [f"{srckey}.{ob}", f"ps.{b_mu}"], [ck], f"{tag}_c{ob}")
                yield
                PO(lambda g, cq_=cq_: g.tensor_mul(sq[si][:], cq_[:], cq_[:]),
                   [ck], [f"sq.{si}"], f"{tag}_s{ob}")
                b_sq = next_bank("att")
                TE(lambda te, b_sq=b_sq: te.matmul(
                    pb[b_sq][0:2, 0:CHUNK], hmaskb, sq[si][:], start=True, stop=True),
                   [f"sq.{si}", "consts"], [f"ps.{b_sq}"], f"{tag}_ss{ob}")
                SC(lambda s, b_sq=b_sq, srs=srs: s.activation(
                    srs[0:2, :], pb[b_sq][0:2, 0:CHUNK], AF.Ln,
                    scale=1.0 / D_HEAD, bias=epsv[0:2, :]),
                   [f"ps.{b_sq}", "epsv"], [rsk], f"{tag}_ln{ob}")
                SC(lambda s, srs=srs: s.activation(srs[0:2, :], srs[0:2, :],
                                                   AF.Exp, scale=-0.5),
                   [rsk], [rsk], f"{tag}_ex{ob}")
                yield
                b_rs = next_bank("att")
                TE(lambda te, b_rs=b_rs, srs=srs: te.matmul(
                    pb[b_rs][:, 0:CHUNK], hmT[:], srs[0:2, :], start=True, stop=True),
                   [rsk, "hmT"], [f"ps.{b_rs}"], f"{tag}_rb{ob}")
                VE(lambda v, b_rs=b_rs, cq_=cq_, tq_=tq_: v.tensor_mul(
                    tq_[:], cq_[:], pb[b_rs][:, 0:CHUNK]),
                   [ck, f"ps.{b_rs}"], [tk], f"{tag}_t{ob}")
                yield
                PO(lambda g, svw=svw, tq_=tq_: g.tensor_scalar(
                    svw, tq_[:], gv, bv, ALU.mult, ALU.add),
                   [tk, "scl"], [f"{srckey}.{ob}"], f"{tag}_o{ob}")
                yield

        def emit_attention(xslot, xkey, tag, extra_gens=()):
            xv = xs[xslot]
            extra = list(extra_gens)

            def step_extras(n=1):
                for _ in range(n):
                    for g in list(extra):
                        try:
                            next(g)
                        except StopIteration:
                            extra.remove(g)
            for mat, dst, dkey in (("wq", qraw, "qraw"), ("wk", kraw, "kraw")):
                for obg in range(2):
                    sl, _ = load_slabs(mat, list(range(obg * 4, obg * 4 + 4)),
                                       f"{tag}{mat}g{obg}")
                    for j in range(4):
                        ob = obg * 4 + j
                        bk = next_bank("att")
                        def mm_q(te, sl=sl, j=j, bk=bk):
                            for kc in range(IC_D):
                                inst = te.matmul(pb[bk][:, 0:CHUNK],
                                                 slab_tile(sl, j, IC_D, kc),
                                                 u[:, kc * CHUNK:(kc + 1) * CHUNK],
                                                 start=(kc == 0), stop=(kc == IC_D - 1))
                            return inst
                        TE(mm_q, [f"wslot.{sl}"] + [f"u.{i}" for i in range(IC_D)],
                           [f"ps.{bk}"], f"{tag}_{mat}m{ob}")
                        ev_eng = SC if mat == "wq" else VE
                        ev_fn = (lambda s, bk=bk, dst=dst, ob=ob: s.copy(
                            dst[:, ob * CHUNK:(ob + 1) * CHUNK],
                            pb[bk][:, 0:CHUNK])) if mat == "wq" else (
                            lambda v, bk=bk, dst=dst, ob=ob: v.tensor_copy(
                                dst[:, ob * CHUNK:(ob + 1) * CHUNK],
                                pb[bk][:, 0:CHUNK]))
                        ev_eng(ev_fn, [f"ps.{bk}"], [f"{dkey}.{ob}"],
                               f"{tag}_{mat}e{ob}")
                        step_extras()

            def gen_vproj():
                for half in range(2):
                    sl, _ = load_slabs("wv", list(range(half * 4, half * 4 + 4)),
                                       f"{tag}wvg{half}")
                    wview = wsl[sl][:, 0:WSLOT_W].rearrange(
                        "p (s rest) -> p s rest", s=4)
                    for ts in range(N_TSUB):
                        bk = next_bank("att")
                        def mm_v(te, sl=sl, bk=bk, ts=ts, wview=wview):
                            for kc in range(IC_D):
                                inst = te.matmul(
                                    pb[bk][:, 0:CHUNK],
                                    u[:, kc * CHUNK + ts * TSUB:kc * CHUNK + (ts + 1) * TSUB],
                                    wview[:, :, kc * 128:(kc + 1) * 128],
                                    start=(kc == 0), stop=(kc == IC_D - 1))
                            return inst
                        TE(mm_v, [f"wslot.{sl}"] + [f"u.{i}" for i in range(IC_D)],
                           [f"ps.{bk}"], f"{tag}_vm{ts}_{half}")
                        SC(lambda s, bk=bk, ts=ts, half=half: s.copy(
                            vtm[:, ts * D_MODEL + half * CHUNK:
                                ts * D_MODEL + (half + 1) * CHUNK],
                            pb[bk][:, 0:CHUNK]),
                           [f"ps.{bk}"],
                           [f"vtm.{ts}.{p}" for p in range(half * 4, half * 4 + 4)],
                           f"{tag}_ve{ts}_{half}")
                        yield

            gens = [gen_qk_headln(qraw, "qraw", qgv, qbv, 0, f"{tag}_qln"),
                    gen_qk_headln(kraw, "kraw", kgv, kbv, 1, f"{tag}_kln"),
                    gen_vproj()]
            while gens:
                nxt = []
                for g in gens:
                    try:
                        next(g)
                        nxt.append(g)
                    except StopIteration:
                        pass
                gens = nxt
                step_extras()
            AW = [512, 384, 256, 128]
            AOFF = [0, 512, 896, 1152]
            b_ds = B_DS
            ds_claimed = False
            for pr in range(N_PAIRS):
                qc = qln[:, pr * CHUNK:(pr + 1) * CHUNK]
                kc = kln[:, pr * CHUNK:(pr + 1) * CHUNK]
                abuf = asbh[pr % 2]
                abk = f"asbh.{pr % 2}"
                b01 = (next_bank("att"), next_bank("att"))
                for h in range(2):
                    r0 = h * 64
                    for sb in range(4):
                        bk = b01[sb % 2]
                        TE(lambda te, kc=kc, qc=qc, r0=r0, sb=sb, bk=bk:
                           te.matmul(pb[bk][:, 0:AW[sb]],
                                     kc[r0:r0 + 64, sb * 128:(sb + 1) * 128],
                                     qc[r0:r0 + 64, sb * 128:CHUNK],
                                     start=True, stop=True),
                           [f"kraw.{pr}", f"qraw.{pr}"], [f"ps.{bk}"],
                           f"{tag}_A{pr}_{h}_{sb}")
                        VE(lambda v, bk=bk, sb=sb, abuf=abuf, h=h:
                           v.tensor_mul(
                               abuf[:, h * 1280 + AOFF[sb]:
                                    h * 1280 + AOFF[sb] + AW[sb]],
                               pb[bk][:, 0:AW[sb]], mext[:, 0:AW[sb]]),
                           [f"ps.{bk}", "mext"], [abk], f"{tag}_Ad{pr}_{h}_{sb}")
                bO = next_bank("att")
                def mm_o(te, pr=pr, qc=qc, bO=bO, abuf=abuf):
                    for h in range(2):
                        r0 = h * 64
                        hd = 2 * pr + h
                        for sb in range(4):
                            vsl = vtm[:, sb * D_MODEL + hd * 64:
                                      sb * D_MODEL + (hd + 1) * 64]
                            te.matmul(pb[bO][r0:r0 + 64, sb * 128:CHUNK],
                                      vsl,
                                      abuf[:, h * 1280 + AOFF[sb]:
                                           h * 1280 + AOFF[sb] + AW[sb]],
                                      start=(sb == 0), stop=False)
                        Sb = SbfE if h == 0 else SbfO
                        inst = te.matmul(pb[bO][r0:r0 + 64, 0:CHUNK],
                                         Sb[:, pr * 64:(pr + 1) * 64], qc,
                                         start=False, stop=True)
                    return inst
                TE(mm_o, [abk, "SbfE", "SbfO", f"qraw.{pr}"]
                   + [f"vtm.{sb}.{pr}" for sb in range(4)],
                   [f"ps.{bO}"], f"{tag}_o{pr}")
                SC(lambda s, bO=bO, pr=pr: s.copy(
                    ofm[:, pr * CHUNK:(pr + 1) * CHUNK], pb[bO][:, 0:CHUNK]),
                   [f"ps.{bO}"], [f"ofm.{pr}"], f"{tag}_oe{pr}")
                bT = next_bank("att")
                def mm_t(te, kc=kc, bT=bT):
                    for sb in range(4):
                        inst = te.transpose(
                            pb_bf[bT][:, sb * 128:(sb + 1) * 128],
                            kc[:, sb * 128:(sb + 1) * 128], identb)
                    return inst
                TE(mm_t, [f"kraw.{pr}", "consts"], [f"ps.{bT}"], f"{tag}_kt{pr}")
                SC(lambda s, bT=bT, pr=pr: s.copy(ktmw[pr % 2][:],
                                                  pb_bf[bT][:, 0:CHUNK]),
                   [f"ps.{bT}"], [f"ktmw.{pr % 2}"], f"{tag}_kte{pr}")
                def mm_ds(te, pr=pr, b_ds=b_ds):
                    for h in range(2):
                        r0 = h * 64
                        hd = 2 * pr + h
                        for sb in range(4):
                            vsl = vtm[:, sb * D_MODEL + hd * 64:
                                      sb * D_MODEL + (hd + 1) * 64]
                            inst = te.matmul(
                                pb[b_ds][r0:r0 + 64, pr * 64:(pr + 1) * 64],
                                ktmw[pr % 2][:, sb * 128 + r0:sb * 128 + r0 + 64],
                                vsl, start=(sb == 0), stop=(sb == 3))
                    return inst
                ds_writes = [f"ps.{b_ds}.{pr}"]
                if not ds_claimed:
                    ds_claimed = True
                    ds_writes.append(f"ps.{b_ds}")
                TE(mm_ds, [f"ktmw.{pr % 2}"]
                   + [f"vtm.{sb}.{pr}" for sb in range(4)],
                   ds_writes, f"{tag}_ds{pr}")
                step_extras(2)
            VE(lambda v, b_ds=b_ds: v.tensor_add(Sf[:], Sf[:], pb[b_ds][:, 0:512]),
               ["Sf", f"ps.{b_ds}"] + [f"ps.{b_ds}.{p}" for p in range(N_PAIRS)],
               ["Sf"], f"{tag}_Sacc")
            SC(lambda s: s.copy(SbfE[0:64, :], Sf[0:64, :]),
               ["Sf"], ["SbfE"], f"{tag}_Sbe")
            SC(lambda s: s.copy(SbfO[64:128, :], Sf[64:128, :]),
               ["Sf"], ["SbfO"], f"{tag}_Sbo")
            step_extras()
            for obg in range(2):
                sl, _ = load_slabs("wo", list(range(obg * 4, obg * 4 + 4)),
                                   f"{tag}wog{obg}")
                for j in range(4):
                    od = obg * 4 + j
                    bk = next_bank("att")
                    def mm_wo(te, sl=sl, j=j, bk=bk):
                        for e in range(IC_D):
                            inst = te.matmul(pb[bk][:, 0:CHUNK],
                                             slab_tile(sl, j, IC_D, e),
                                             ofm[:, e * CHUNK:(e + 1) * CHUNK],
                                             start=(e == 0), stop=(e == IC_D - 1))
                        return inst
                    TE(mm_wo, [f"wslot.{sl}"] + [f"ofm.{i}" for i in range(IC_D)],
                       [f"ps.{bk}"], f"{tag}_wom{od}")
                    VE(lambda v, bk=bk, od=od: v.tensor_add(
                        xv[:, od * CHUNK:(od + 1) * CHUNK],
                        xv[:, od * CHUNK:(od + 1) * CHUNK], pb[bk][:, 0:CHUNK]),
                       [f"ps.{bk}", f"{xkey}.{od}"], [f"{xkey}.{od}"],
                       f"{tag}_xr{od}")
                    step_extras()

        xT_r = xT.rearrange("(ic p) t -> p ic t", p=128)
        outT_r = outT.rearrange("(ic p) t -> p ic t", p=128)

        def xload(c):
            slot = c % 2
            xkey = f"x.{slot}"
            DMA("sy", lambda g, c=c, slot=slot: g.dma_start(
                out=xs[slot][:].rearrange("p (ic t) -> p ic t", t=CHUNK),
                in_=xT_r[:, :, c * CHUNK:(c + 1) * CHUNK]),
                [], [xkey] + [f"{xkey}.{od}" for od in range(IC_D)],
                f"xl{slot}", f"xload{c}")

        def xstore(c):
            slot = c % 2
            xkey = f"x.{slot}"
            DMA("sy", lambda g, c=c, slot=slot: g.dma_start(
                out=outT_r[:, :, c * CHUNK:(c + 1) * CHUNK],
                in_=xs[slot][:].rearrange("p (ic t) -> p ic t", t=CHUNK)),
                [xkey] + [f"{xkey}.{od}" for od in range(IC_D)],
                [f"stout.{c}"], f"st{slot}", f"xstore{c}")

        full = set(phases) == {"f1", "att", "f2"}
        if not full:
            for c in range(n_chunks):
                slot, xkey = c % 2, f"x.{c % 2}"
                xload(c)
                if "f1" in phases:
                    drain(gen_layernorm(slot, xkey, uf, "uf", f"c{c}ln1"))
                    drain(gen_gate(0, uf, "uf", f"c{c}f1"))
                    emit_hw3(0, slot, xkey, f"c{c}f1")
                if "att" in phases:
                    drain(gen_layernorm(slot, xkey, u, "u", f"c{c}ln2"))
                    emit_attention(slot, xkey, f"c{c}at")
                if "f2" in phases:
                    drain(gen_layernorm(slot, xkey, uf, "uf", f"c{c}ln3"))
                    drain(gen_gate(1, uf, "uf", f"c{c}f2"))
                    emit_hw3(1, slot, xkey, f"c{c}f2")
                xstore(c)
        else:
            xload(0)
            drain(gen_layernorm(0, "x.0", uf, "uf", "c0ln1"))
            drain(gen_gate(0, uf, "uf", "c0f1"))
            emit_hw3(0, 0, "x.0", "c0f1")
            for c in range(n_chunks):
                slot, xkey = c % 2, f"x.{c % 2}"
                nslot, nxkey = (c + 1) % 2, f"x.{(c + 1) % 2}"
                drain(gen_layernorm(slot, xkey, u, "u", f"c{c}ln2"))
                extras = []
                if c + 1 < n_chunks:
                    xload(c + 1)
                    extras = [gen_layernorm(nslot, nxkey, uf, "uf",
                                            f"c{c + 1}ln1", use_pool=True),
                              gen_gate(0, uf, "uf", f"c{c + 1}f1")]
                emit_attention(slot, xkey, f"c{c}at", extra_gens=extras)
                drain(gen_layernorm(slot, xkey, uf, "uf", f"c{c}ln3"))
                drain(gen_gate(1, uf, "uf", f"c{c}f2"))
                emit_hw3(1, slot, xkey, f"c{c}f2")
                xstore(c)
                if c + 1 < n_chunks:
                    emit_hw3(0, nslot, nxkey, f"c{c + 1}f1")

        final_waits = []
        for sname in ("st0", "st1"):
            if P.sem_count.get(sname, 0):
                final_waits.append((sname, P.sem_count[sname]))

        block = E(nc.Block())

        def replay(engine_name):
            def fn(eng):
                for op in P.ops[engine_name]:
                    for s, v in op.waits:
                        eng.wait_ge(sems[s], v)
                    inst = op.thunk(eng)
                    if op.inc is not None:
                        inst.then_inc(sems[op.inc[0]], op.inc[1])
                if engine_name == "sy":
                    for s, v in final_waits:
                        eng.wait_ge(sems[s], v)
            return fn

        block.tensor(replay("te"))
        block.scalar(replay("sc"))
        block.vector(replay("ve"))
        block.gpsimd(replay("po"))
        block.sync(replay("sy"))

    return nc



def _slab_tile(wT):
    K, O = wT.shape
    return np.ascontiguousarray(
        wT.reshape(K // 128, 128, O // 128, 128).transpose(2, 1, 0, 3))


def _scales(s):
    n = s.shape[0] // 128
    return np.ascontiguousarray(s.reshape(n, 128).T)


def make_weight_inputs(ins):
    g = lambda k: np.asarray(ins[k], np.float32)
    wm = {
        "w1": _slab_tile(g("L0_ffn_w1_w").T),
        "w2": _slab_tile(g("L0_ffn_w2_w").T),
        "h1": _slab_tile(g("L0_ffn_h1_w").T),
        "h2": _slab_tile(g("L0_ffn_h2_w").T),
        "w3": _slab_tile(g("L0_ffn_w3_w").T),
        "wq": _slab_tile(g("L0_attn_Wq").T),
        "wk": _slab_tile(g("L0_attn_Wk").T),
        "wv": _slab_tile(g("L0_attn_Wv").T),
        "wo": _slab_tile(g("L0_attn_Wo").T),
        "s1": _scales(g("L0_ffn_w1_s")),
        "s2": _scales(g("L0_ffn_w2_s")),
        "sh1": _scales(g("L0_ffn_h1_s")),
        "sh2": _scales(g("L0_ffn_h2_s")),
        "s3": _scales(g("L0_ffn_w3_s")),
        "qg": np.ascontiguousarray(np.tile(g("L0_attn_q_gamma"), 2)[:, None]),
        "qb": np.ascontiguousarray(np.tile(g("L0_attn_q_beta"), 2)[:, None]),
        "kg": np.ascontiguousarray(np.tile(g("L0_attn_k_gamma"), 2)[:, None]),
        "kb": np.ascontiguousarray(np.tile(g("L0_attn_k_beta"), 2)[:, None]),
        "identc": np.eye(128, dtype=np.float32),
        "maskc": np.ascontiguousarray(np.tril(np.ones((128, 128), np.float32)).T),
        "mextc": np.ascontiguousarray(np.concatenate(
            [np.tril(np.ones((128, 128), np.float32)).T,
             np.ones((128, 384), np.float32)], axis=1)),
        "onesc": np.ones((128, 1), np.float32),
        "onesrow": np.ones((1, 128), np.float32),
        "hmask": np.ascontiguousarray(
            np.stack([np.arange(128) < 64, np.arange(128) >= 64], 1)
            .astype(np.float32)),
        "hmaskT": np.ascontiguousarray(
            np.stack([np.arange(128) < 64, np.arange(128) >= 64], 0)
            .astype(np.float32)),
    }
    return wm


def kernel(**inputs):
    wm = make_weight_inputs(inputs)
    x = np.asarray(inputs["L0_attn_x"], np.float32)
    nc = build_program()
    in_maps = []
    for b in range(N_CORES):
        m = dict(wm)
        m["xT"] = np.ascontiguousarray(x[b].T)
        in_maps.append(m)
    res = run_bass_kernel_spmd(nc, in_maps, core_ids=list(range(N_CORES)))
    out = np.empty((N_CORES, T_FULL, D_MODEL), np.float32)
    for b in range(N_CORES):
        out[b] = res.results[b]["outT"].T
    return out


# version 16
# speedup vs baseline: 1.0016x; 1.0016x over previous
import numpy as np
from contextlib import ExitStack

import concourse.bass as bass
import concourse.mybir as mybir
from concourse.bass_utils import run_bass_kernel_spmd

F32 = mybir.dt.float32
BF16 = mybir.dt.bfloat16
AF = mybir.ActivationFunctionType
ALU = mybir.AluOpType

D_MODEL = 1024
D_HID = 4096
T_FULL = 2048
N_CORES = 8
CHUNK = 512
N_CHUNKS = T_FULL // CHUNK
TSUB = 128
N_TSUB = CHUNK // TSUB
N_HEADS = 16
D_HEAD = 64
N_PAIRS = 8
EPS = 1e-5

IC_D = D_MODEL // 128
IC_H = D_HID // 128

N_WSLOTS = 4
WSLOT_W = 4096



class Op:
    __slots__ = ("thunk", "waits", "inc", "tag")

    def __init__(self, thunk, waits, inc, tag):
        self.thunk = thunk
        self.waits = waits
        self.inc = inc
        self.tag = tag


class Prog:

    ENGINES = ("te", "sc", "ve", "po", "sy")

    def __init__(self):
        self.ops = {e: [] for e in self.ENGINES}
        self.sem_count = {}
        self.eng_sem = {"te": "s_te", "sc": "s_sc", "ve": "s_ve",
                        "po": "s_po", "sy": "s_sy"}
        for s in self.eng_sem.values():
            self.sem_count[s] = 0
        self.buf = {}
        self.knowledge = {e: {} for e in self.ENGINES}

    def _b(self, key):
        if key not in self.buf:
            self.buf[key] = {"w": None, "r": []}
        return self.buf[key]

    def emit(self, eng, thunk, reads=(), writes=(), dma_sem=None, tag=""):
        waits = {}

        def dep(token):
            if token is None:
                return
            s, v = token
            if waits.get(s, 0) < v:
                waits[s] = v

        for k in reads:
            dep(self._b(k)["w"])
        for k in writes:
            b = self._b(k)
            dep(b["w"])
            for r in b["r"]:
                dep(r)
        kn = self.knowledge[eng]
        wl = []
        for s, v in waits.items():
            if kn.get(s, 0) >= v:
                continue
            wl.append((s, v))
            kn[s] = v
        if dma_sem is not None:
            self.sem_count.setdefault(dma_sem, 0)
            self.sem_count[dma_sem] += 16
            token = (dma_sem, self.sem_count[dma_sem])
            inc = (dma_sem, 16)
        else:
            s = self.eng_sem[eng]
            self.sem_count[s] += 1
            token = (s, self.sem_count[s])
            inc = (s, 1)
        for k in reads:
            self._b(k)["r"].append(token)
        for k in writes:
            b = self._b(k)
            b["w"] = token
            b["r"] = []
        self.ops[eng].append(Op(thunk, wl, inc, tag))
        return token

    def barrier_token(self, key):
        return self._b(key)["w"]

    def n_instructions(self):
        return sum(len(v) for v in self.ops.values())



def build_program(n_chunks=N_CHUNKS, phases=("f1", "att", "f2")):
    nc = bass.Bass("TRN2", target_bir_lowering=False)

    dt_in = {}

    def din(name, shape):
        dt_in[name] = nc.dram_tensor(name, shape, F32, kind="ExternalInput").ap()
        return dt_in[name]

    xT = din("xT", [D_MODEL, T_FULL])
    w1 = din("w1", [IC_H, 128, IC_D, 128])
    w2 = din("w2", [IC_H, 128, IC_D, 128])
    h1 = din("h1", [IC_H, 128, IC_H, 128])
    h2 = din("h2", [IC_H, 128, IC_H, 128])
    w3 = din("w3", [IC_D, 128, IC_H, 128])
    wq = din("wq", [IC_D, 128, IC_D, 128])
    wk = din("wk", [IC_D, 128, IC_D, 128])
    wv = din("wv", [IC_D, 128, IC_D, 128])
    wo = din("wo", [IC_D, 128, IC_D, 128])
    s1 = din("s1", [128, IC_H])
    s2 = din("s2", [128, IC_H])
    sh1 = din("sh1", [128, IC_H])
    sh2 = din("sh2", [128, IC_H])
    s3 = din("s3", [128, IC_D])
    qg = din("qg", [128, 1])
    qb = din("qb", [128, 1])
    kg = din("kg", [128, 1])
    kb = din("kb", [128, 1])
    identc = din("identc", [128, 128])
    maskc = din("maskc", [128, 128])
    mextc = din("mextc", [128, CHUNK])
    onesc = din("onesc", [128, 1])
    onesrow = din("onesrow", [1, 128])
    hmask = din("hmask", [128, 2])
    hmaskT = din("hmaskT", [2, 128])
    outT = nc.dram_tensor("outT", [D_MODEL, T_FULL], F32, kind="ExternalOutput").ap()

    scr = {}
    for nm, t in (("w1", w1), ("w2", w2), ("h1", h1), ("h2", h2), ("w3", w3),
                  ("wq", wq), ("wk", wk), ("wv", wv), ("wo", wo)):
        scr[nm] = nc.dram_tensor("scr_" + nm, list(t.tensor.shape), BF16,
                                 kind="Internal").ap()

    P = Prog()

    with ExitStack() as stack:
        E = stack.enter_context

        xs = [E(nc.sbuf_tensor(f"xs{i}", [128, IC_D * CHUNK], F32)) for i in range(2)]
        u = E(nc.sbuf_tensor("u", [128, IC_D * CHUNK], BF16))
        hA = E(nc.sbuf_tensor("hA", [128, IC_H * CHUNK], BF16))
        hB = E(nc.sbuf_tensor("hB", [128, IC_H * CHUNK], BF16))
        wsl = [E(nc.sbuf_tensor(f"wsl{i}", [128, WSLOT_W], BF16)) for i in range(N_WSLOTS)]
        sq = [E(nc.sbuf_tensor(f"sq{i}", [128, CHUNK], BF16)) for i in range(2)]
        cqs = [E(nc.sbuf_tensor(f"cq{i}", [128, CHUNK], BF16)) for i in range(2)]
        tqs = [E(nc.sbuf_tensor(f"tq{i}", [128, CHUNK], BF16)) for i in range(2)]
        srow_mu = [E(nc.sbuf_tensor(f"srmu{i}", [2, CHUNK], BF16)) for i in range(2)]
        srow_rs = [E(nc.sbuf_tensor(f"srrs{i}", [2, CHUNK], BF16)) for i in range(2)]
        Sf = E(nc.sbuf_tensor("Sf", [128, N_PAIRS * D_HEAD], F32))
        SbfE = E(nc.sbuf_tensor("SbfE", [128, N_PAIRS * D_HEAD], BF16))
        SbfO = E(nc.sbuf_tensor("SbfO", [128, N_PAIRS * D_HEAD], BF16))
        asbh = [E(nc.sbuf_tensor(f"asbh{i}", [128, 2 * 1280], BF16))
                for i in range(2)]
        ktmw = [E(nc.sbuf_tensor(f"ktmw{i}", [128, CHUNK], BF16)) for i in range(2)]
        strow = E(nc.sbuf_tensor("strow", [2, CHUNK], BF16))
        strow2 = strow
        consts = E(nc.sbuf_tensor("consts", [128, 128 + 128 + 1 + 2], BF16))
        mext = E(nc.sbuf_tensor("mext", [128, CHUNK], BF16))
        musb = E(nc.sbuf_tensor("musb", [128, CHUNK], BF16))
        rsb = E(nc.sbuf_tensor("rsb", [128, CHUNK], BF16))
        onesr = E(nc.sbuf_tensor("onesr", [1, 128], BF16))
        hmT = E(nc.sbuf_tensor("hmT", [2, 128], BF16))
        scl = E(nc.sbuf_tensor("scl", [128, 4 * IC_H + 2 * IC_D + 4], F32))
        epsv = E(nc.sbuf_tensor("epsv", [128, 1], F32))
        onesf = E(nc.sbuf_tensor("onesf", [128, 1], F32))
        cen = hB[:, 0:IC_D * CHUNK]
        xbf = cen
        OVL = E(nc.sbuf_tensor("OVL", [128, 4 * IC_D * CHUNK], BF16))
        qraw = OVL[:, 0:4096]
        kraw = OVL[:, 4096:8192]
        vtm = OVL[:, 8192:12288]
        ofm = OVL[:, 12288:16384]
        qln, kln = qraw, kraw
        uf = E(nc.sbuf_tensor("uf", [128, IC_D * CHUNK], BF16))
        hG = [hA, hB]

        identb = consts[:, 0:128]
        maskb = consts[:, 128:256]
        onescol = consts[:, 256:257]
        hmaskb = consts[:, 257:259]
        s1v = scl[:, 0:IC_H]
        s2v = scl[:, IC_H:2 * IC_H]
        sh1v = scl[:, 2 * IC_H:3 * IC_H]
        sh2v = scl[:, 3 * IC_H:4 * IC_H]
        s3v = scl[:, 4 * IC_H:4 * IC_H + IC_D]
        s3hv = scl[:, 4 * IC_H + IC_D:4 * IC_H + 2 * IC_D]
        qgv = scl[:, 4 * IC_H + 2 * IC_D + 0:4 * IC_H + 2 * IC_D + 1]
        qbv = scl[:, 4 * IC_H + 2 * IC_D + 1:4 * IC_H + 2 * IC_D + 2]
        kgv = scl[:, 4 * IC_H + 2 * IC_D + 2:4 * IC_H + 2 * IC_D + 3]
        kbv = scl[:, 4 * IC_H + 2 * IC_D + 3:4 * IC_H + 2 * IC_D + 4]

        pb = [E(nc.psum_tensor(f"pb{i}", [128, 512], F32)) for i in range(8)]
        pb_bf = [b[:].tensor.bitcast(BF16).ap() for b in pb]

        sem_names = (["s_te", "s_sc", "s_ve", "s_po", "s_sy"]
                     + [f"wsl{i}" for i in range(N_WSLOTS)]
                     + [f"wslc{i}" for i in range(N_WSLOTS)]
                     + [f"wst{i}" for i in range(N_WSLOTS)]
                     + ["xl0", "xl1", "st0", "st1", "cst"])
        sems = {n: E(nc.semaphore(n)) for n in sem_names}


        def TE(thunk, reads, writes, tag=""):
            return P.emit("te", thunk, reads, writes, tag=tag)

        def SC(thunk, reads, writes, tag=""):
            return P.emit("sc", thunk, reads, writes, tag=tag)

        def VE(thunk, reads, writes, tag=""):
            return P.emit("ve", thunk, reads, writes, tag=tag)

        def PO(thunk, reads, writes, tag=""):
            return P.emit("po", thunk, reads, writes, tag=tag)

        def DMA(q, thunk, reads, writes, sem, tag=""):
            return P.emit(q, thunk, reads, writes, dma_sem=sem, tag=tag)

        DMA("po", lambda g: g.dma_start(out=consts[:, 0:128], in_=identc),
            [], ["consts", "cstall"], "cst", "ident")
        DMA("po", lambda g: g.dma_start(out=mext[:], in_=mextc),
            [], ["mext", "cstall"], "cst", "mext")
        DMA("po", lambda g: g.dma_start(out=consts[:, 256:257], in_=onesc),
            [], ["consts", "cstall"], "cst", "ones")
        DMA("po", lambda g: g.dma_start(out=onesr[:], in_=onesrow),
            [], ["onesr", "cstall"], "cst", "onesrow")
        DMA("po", lambda g: g.dma_start(out=hmT[:], in_=hmaskT),
            [], ["hmT", "cstall"], "cst", "hmT")
        DMA("po", lambda g: g.dma_start(out=consts[:, 257:259], in_=hmask),
            [], ["consts", "cstall"], "cst", "hmask")
        DMA("po", lambda g: g.dma_start(out=scl[:, 0:IC_H], in_=s1),
            [], ["scl", "cstall"], "cst", "s1")
        DMA("po", lambda g: g.dma_start(out=scl[:, IC_H:2 * IC_H], in_=s2),
            [], ["scl", "cstall"], "cst", "s2")
        DMA("po", lambda g: g.dma_start(out=scl[:, 2 * IC_H:3 * IC_H], in_=sh1),
            [], ["scl", "cstall"], "cst", "sh1")
        DMA("po", lambda g: g.dma_start(out=scl[:, 3 * IC_H:4 * IC_H], in_=sh2),
            [], ["scl", "cstall"], "cst", "sh2")
        DMA("po", lambda g: g.dma_start(out=scl[:, 4 * IC_H:4 * IC_H + IC_D], in_=s3),
            [], ["scl", "cstall"], "cst", "s3")
        base = 4 * IC_H + 2 * IC_D
        for i, t in enumerate((qg, qb, kg, kb)):
            DMA("po", lambda g, i=i, t=t: g.dma_start(
                out=scl[:, base + i:base + i + 1], in_=t),
                [], ["scl", "cstall"], "cst", "qkgb")
        VE(lambda v: v.memset(epsv[:], EPS), [], ["epsv"], "eps")
        VE(lambda v: v.memset(onesf[:], 1.0), [], ["onesf"], "onesf")
        VE(lambda v: v.tensor_scalar_mul(s3hv, s3v, 0.5), ["scl"], ["scl"], "s3h")
        VE(lambda v: v.memset(Sf[:], 0.0), [], ["Sf"], "S0")
        VE(lambda v: v.memset(SbfE[:], 0.0), [], ["SbfE"], "SbfE0")
        VE(lambda v: v.memset(SbfO[:], 0.0), [], ["SbfO"], "SbfO0")

        ring_state = {"next": 0}
        fp32_src = {"w1": w1, "w2": w2, "h1": h1, "h2": h2, "w3": w3,
                    "wq": wq, "wk": wk, "wv": wv, "wo": wo}
        scr_built = set()

        def load_slabs(mat, obs, tag):
            n_kc = {"w1": IC_D, "w2": IC_D, "h1": IC_H, "h2": IC_H,
                    "w3": IC_H, "wq": IC_D, "wk": IC_D, "wv": IC_D,
                    "wo": IC_D}[mat]
            slot = ring_state["next"]
            ring_state["next"] = (slot + 1) % N_WSLOTS
            w_per_slab = n_kc * 128
            assert w_per_slab * len(obs) <= WSLOT_W
            gkey = f"scrg_{mat}.{obs[0]}"
            sbuf_view = wsl[slot][:, 0:w_per_slab * len(obs)].rearrange(
                "p (s ko) -> p s ko", s=len(obs))
            if (mat, obs[0]) not in scr_built:
                scr_built.add((mat, obs[0]))
                src_ap = fp32_src[mat]

                def thunk(g, src_ap=src_ap, obs=obs, sbuf_view=sbuf_view):
                    o0 = obs[0]
                    return g.dma_start(
                        out=sbuf_view,
                        in_=src_ap[o0:o0 + len(obs)].rearrange(
                            "s p k o -> p s (k o)"))
                DMA("po", thunk, [], [f"wslot.{slot}"], f"wslc{slot}",
                    f"cld_{tag}")
                sc_ap = scr[mat]

                def sthunk(g, sc_ap=sc_ap, obs=obs, sbuf_view=sbuf_view):
                    o0 = obs[0]
                    return g.dma_start(
                        out=sc_ap[o0:o0 + len(obs)].rearrange(
                            "s p k o -> p s (k o)"),
                        in_=sbuf_view)
                DMA("sy", sthunk, [f"wslot.{slot}"], [gkey],
                    f"wst{slot}", f"sst_{tag}")
            else:
                sc_ap = scr[mat]

                def thunk(g, sc_ap=sc_ap, obs=obs, sbuf_view=sbuf_view):
                    o0 = obs[0]
                    return g.dma_start(
                        out=sbuf_view,
                        in_=sc_ap[o0:o0 + len(obs)].rearrange(
                            "s p k o -> p s (k o)"))
                DMA("sy", thunk, [gkey], [f"wslot.{slot}"], f"wsl{slot}",
                    f"ld_{tag}")
            return slot, n_kc

        def slab_tile(slot, slab_i, n_kc, kc):
            off = (slab_i * n_kc + kc) * 128
            return wsl[slot][:, off:off + 128]

        B_DS = 4
        psum_rr = {"att": 0, "ffn": 0}
        POOLS = {"att": (0, 1, 2, 3), "ffn": (5, 6, 7)}

        def next_bank(pool="ffn", exclude=()):
            ring = POOLS[pool]
            for _ in range(len(ring)):
                i = ring[psum_rr[pool] % len(ring)]
                psum_rr[pool] += 1
                if i not in exclude:
                    return i
            raise RuntimeError

        def ovl_keys(ob):
            if ob < 8:
                return [f"qraw.{ob}"]
            if ob < 16:
                return [f"kraw.{ob - 8}"]
            if ob < 24:
                c0 = (ob - 16) * 512
                p0 = (c0 % 1024) // 128
                return [f"vtm.{c0 // 1024}.{p}" for p in range(p0, p0 + 4)]
            return [f"ofm.{ob - 24}"]
        ALL_OVL = [k for ob in range(IC_H) for k in ovl_keys(ob)]

        def hh_view(ob):
            return OVL[:, ob * CHUNK:(ob + 1) * CHUNK]


        def drain(g):
            for _ in g:
                pass

        def gen_layernorm(xslot, xkey, ubuf, ukp, tag, use_pool=False):
            xv = xs[xslot]
            b_st = next_bank("ffn")
            def mm_mu(te):
                for ic in range(IC_D):
                    inst = te.matmul(pb[b_st][0:1, 0:CHUNK], onesf[:],
                                     xv[:, ic * CHUNK:(ic + 1) * CHUNK],
                                     start=(ic == 0), stop=(ic == IC_D - 1))
                return inst
            TE(mm_mu, [f"{xkey}.{ic}" for ic in range(IC_D)] + ["onesf"],
               [f"ps.{b_st}"], f"{tag}_musum")
            yield
            SC(lambda s: s.activation(strow[0:1, :], pb[b_st][0:1, 0:CHUNK],
                                      AF.Copy, scale=1.0 / D_MODEL),
               [f"ps.{b_st}"], ["strow"], f"{tag}_murow")
            b_mu = next_bank("ffn")
            TE(lambda te: te.matmul(pb[b_mu][:, 0:CHUNK], onesr[:], strow[0:1, :],
                                    start=True, stop=True),
               ["strow", "onesr"], [f"ps.{b_mu}"], f"{tag}_mubc")
            SC(lambda s: s.copy(musb[:], pb[b_mu][:, 0:CHUNK]),
               [f"ps.{b_mu}"], ["musb"], f"{tag}_musb")
            yield
            b_sq = next_bank("ffn")
            mub = musb[:].rearrange("p (a t) -> p a t", a=1).broadcast_to(
                [128, 4, CHUNK])
            for hf, eng in ((0, VE), (1, PO)):
                eng(lambda e, hf=hf: e.tensor_sub(
                    cen[:, hf * 4 * CHUNK:(hf + 1) * 4 * CHUNK].rearrange(
                        "p (i t) -> p i t", t=CHUNK),
                    xv[:, hf * 4 * CHUNK:(hf + 1) * 4 * CHUNK].rearrange(
                        "p (i t) -> p i t", t=CHUNK), mub),
                    [f"{xkey}.{ic}" for ic in range(hf * 4, hf * 4 + 4)]
                    + ["musb"],
                    [f"cen.h{hf}", "xbf"]
                    + [f"hg1.{ic}" for ic in range(hf * 4, hf * 4 + 4)],
                    f"{tag}_cen{hf}")
            for ic in range(IC_D):
                eng2 = PO if ic % 2 == 0 else VE
                eng2(lambda e, ic=ic: e.tensor_mul(
                    sq[ic % 2][:], cen[:, ic * CHUNK:(ic + 1) * CHUNK],
                    cen[:, ic * CHUNK:(ic + 1) * CHUNK]),
                    [f"cen.h{ic // 4}", "xbf"], [f"sq.{ic % 2}"], f"{tag}_sq{ic}")
                if ic % 2 == 1:
                    def mm_sq(te, ic=ic):
                        for j in (ic - 1, ic):
                            inst = te.matmul(pb[b_sq][0:1, 0:CHUNK], onescol,
                                             sq[j % 2][:],
                                             start=(j == 0), stop=(j == IC_D - 1))
                        return inst
                    TE(mm_sq, ["sq.0", "sq.1", "consts"],
                       [f"ps.{b_sq}"], f"{tag}_sqs{ic}")
            yield
            SC(lambda s: s.activation(strow2[0:1, :], pb[b_sq][0:1, 0:CHUNK],
                                      AF.Ln, scale=1.0 / D_MODEL, bias=epsv[0:1, :]),
               [f"ps.{b_sq}", "epsv"], ["strow"], f"{tag}_ln")
            SC(lambda s: s.activation(strow2[0:1, :], strow2[0:1, :],
                                      AF.Exp, scale=-0.5),
               ["strow"], ["strow"], f"{tag}_exp")
            b_rs = next_bank("ffn")
            TE(lambda te: te.matmul(pb[b_rs][:, 0:CHUNK], onesr[:],
                                    strow2[0:1, :], start=True, stop=True),
               ["strow", "onesr"], [f"ps.{b_rs}"], f"{tag}_rsbc")
            SC(lambda s: s.copy(rsb[:], pb[b_rs][:, 0:CHUNK]),
               [f"ps.{b_rs}"], ["rsb"], f"{tag}_rsb")
            yield
            rsbb = rsb[:].rearrange("p (a t) -> p a t", a=1).broadcast_to(
                [128, 4, CHUNK])
            for hf, eng in ((0, VE), (1, PO)):
                eng(lambda e, hf=hf, ubuf=ubuf: e.tensor_mul(
                    ubuf[:, hf * 4 * CHUNK:(hf + 1) * 4 * CHUNK].rearrange(
                        "p (i t) -> p i t", t=CHUNK),
                    cen[:, hf * 4 * CHUNK:(hf + 1) * 4 * CHUNK].rearrange(
                        "p (i t) -> p i t", t=CHUNK), rsbb),
                    [f"cen.h{hf}", "xbf", "rsb"],
                    [f"{ukp}.{ic}" for ic in range(hf * 4, hf * 4 + 4)],
                    f"{tag}_u{hf}")

        def gen_gate(gpar, ubuf, ukp, tag):
            G = hG[gpar]
            gk = f"hg{gpar}"
            for obg in range(IC_H // 4):
                sl1, _ = load_slabs("w1", list(range(obg * 4, obg * 4 + 4)), f"{tag}w1g{obg}")
                sl2, _ = load_slabs("w2", list(range(obg * 4, obg * 4 + 4)), f"{tag}w2g{obg}")
                for j in range(4):
                    ob = obg * 4 + j
                    ba = next_bank("ffn")
                    def mm_a(te, sl1=sl1, j=j, ba=ba, ubuf=ubuf):
                        for kc in range(IC_D):
                            inst = te.matmul(pb[ba][:, 0:CHUNK],
                                             slab_tile(sl1, j, IC_D, kc),
                                             ubuf[:, kc * CHUNK:(kc + 1) * CHUNK],
                                             start=(kc == 0), stop=(kc == IC_D - 1))
                        return inst
                    TE(mm_a, [f"wslot.{sl1}"] + [f"{ukp}.{i}" for i in range(IC_D)],
                       [f"ps.{ba}"], f"{tag}_a{ob}")
                    SC(lambda s, ba=ba, ob=ob, G=G: s.activation(
                        G[:, ob * CHUNK:(ob + 1) * CHUNK], pb[ba][:, 0:CHUNK],
                        AF.Silu, scale=s1v[:, ob:ob + 1]),
                       [f"ps.{ba}", "scl"], [f"{gk}.{ob}"], f"{tag}_silu{ob}")
                    bb = next_bank("ffn")
                    def mm_b(te, sl2=sl2, j=j, bb=bb, ubuf=ubuf):
                        for kc in range(IC_D):
                            inst = te.matmul(pb[bb][:, 0:CHUNK],
                                             slab_tile(sl2, j, IC_D, kc),
                                             ubuf[:, kc * CHUNK:(kc + 1) * CHUNK],
                                             start=(kc == 0), stop=(kc == IC_D - 1))
                        return inst
                    TE(mm_b, [f"wslot.{sl2}"] + [f"{ukp}.{i}" for i in range(IC_D)],
                       [f"ps.{bb}"], f"{tag}_b{ob}")
                    VE(lambda v, bb=bb, ob=ob, G=G: v.scalar_tensor_tensor(
                        G[:, ob * CHUNK:(ob + 1) * CHUNK], pb[bb][:, 0:CHUNK],
                        s2v[:, ob:ob + 1], G[:, ob * CHUNK:(ob + 1) * CHUNK],
                        ALU.mult, ALU.mult),
                       [f"ps.{bb}", "scl", f"{gk}.{ob}"], [f"{gk}.{ob}"],
                       f"{tag}_gate{ob}")
                    yield

        def emit_hw3(gpar, xslot, xkey, tag):
            xv = xs[xslot]
            G = hG[gpar]
            gk = f"hg{gpar}"
            for ob in range(IC_H):
                sl, _ = load_slabs("h1", [ob], f"{tag}h1s{ob}")
                bk = next_bank("ffn")
                def mm_h1(te, sl=sl, bk=bk, G=G):
                    for kc in range(IC_H):
                        inst = te.matmul(pb[bk][:, 0:CHUNK],
                                         slab_tile(sl, 0, IC_H, kc),
                                         G[:, kc * CHUNK:(kc + 1) * CHUNK],
                                         start=(kc == 0), stop=(kc == IC_H - 1))
                    return inst
                TE(mm_h1, [f"wslot.{sl}"] + [f"{gk}.{i}" for i in range(IC_H)],
                   [f"ps.{bk}"], f"{tag}_h1m{ob}")
                SC(lambda s, bk=bk, ob=ob: s.activation(
                    hh_view(ob), pb[bk][:, 0:CHUNK],
                    AF.Gelu, scale=sh1v[:, ob:ob + 1]),
                   [f"ps.{bk}", "scl"], ovl_keys(ob), f"{tag}_h1g{ob}")
            for ob in range(IC_H):
                sl, _ = load_slabs("h2", [ob], f"{tag}h2s{ob}")
                bk = next_bank("ffn")
                def mm_h2(te, sl=sl, bk=bk):
                    for kc in range(IC_H):
                        inst = te.matmul(pb[bk][:, 0:CHUNK],
                                         slab_tile(sl, 0, IC_H, kc),
                                         hh_view(kc),
                                         start=(kc == 0), stop=(kc == IC_H - 1))
                    return inst
                TE(mm_h2, [f"wslot.{sl}"] + ALL_OVL,
                   [f"ps.{bk}"], f"{tag}_h2m{ob}")
                SC(lambda s, bk=bk, ob=ob, G=G: s.activation(
                    G[:, ob * CHUNK:(ob + 1) * CHUNK], pb[bk][:, 0:CHUNK],
                    AF.Gelu, scale=sh2v[:, ob:ob + 1]),
                   [f"ps.{bk}", "scl"], [f"{gk}.{ob}"], f"{tag}_h2g{ob}")
            for ob in range(IC_D):
                sl, _ = load_slabs("w3", [ob], f"{tag}w3s{ob}")
                bk = next_bank("ffn")
                def mm_w3(te, sl=sl, bk=bk, G=G):
                    for kc in range(IC_H):
                        inst = te.matmul(pb[bk][:, 0:CHUNK],
                                         slab_tile(sl, 0, IC_H, kc),
                                         G[:, kc * CHUNK:(kc + 1) * CHUNK],
                                         start=(kc == 0), stop=(kc == IC_H - 1))
                    return inst
                TE(mm_w3, [f"wslot.{sl}"] + [f"{gk}.{i}" for i in range(IC_H)],
                   [f"ps.{bk}"], f"{tag}_w3m{ob}")
                VE(lambda v, bk=bk, ob=ob: v.scalar_tensor_tensor(
                    xv[:, ob * CHUNK:(ob + 1) * CHUNK], pb[bk][:, 0:CHUNK],
                    s3hv[:, ob:ob + 1], xv[:, ob * CHUNK:(ob + 1) * CHUNK],
                    ALU.mult, ALU.add),
                   [f"ps.{bk}", "scl", f"{xkey}.{ob}"], [f"{xkey}.{ob}"],
                   f"{tag}_res{ob}")

        def gen_qk_headln(src_t, srckey, gv, bv, si, tag):
            cq_, tq_ = cqs[si], tqs[si]
            smu, srs = srow_mu[si], srow_rs[si]
            ck, tk, muk, rsk = f"cq{si}", f"tq{si}", f"smu{si}", f"srs{si}"
            for ob in range(IC_D):
                svw = src_t[:, ob * CHUNK:(ob + 1) * CHUNK]
                b_st = next_bank("att")
                TE(lambda te, svw=svw, b_st=b_st: te.matmul(
                    pb[b_st][0:2, 0:CHUNK], hmaskb, svw, start=True, stop=True),
                   [f"{srckey}.{ob}", "consts"], [f"ps.{b_st}"], f"{tag}_mu{ob}")
                SC(lambda s, b_st=b_st, smu=smu: s.activation(
                    smu[:, :], pb[b_st][0:2, 0:CHUNK], AF.Copy, scale=1.0 / D_HEAD),
                   [f"ps.{b_st}"], [muk], f"{tag}_mur{ob}")
                b_mu = next_bank("att")
                TE(lambda te, b_mu=b_mu, smu=smu: te.matmul(
                    pb[b_mu][:, 0:CHUNK], hmT[:], smu[:, :], start=True, stop=True),
                   [muk, "hmT"], [f"ps.{b_mu}"], f"{tag}_mub{ob}")
                VE(lambda v, svw=svw, b_mu=b_mu, cq_=cq_: v.tensor_sub(
                    cq_[:], svw, pb[b_mu][:, 0:CHUNK]),
                   [f"{srckey}.{ob}", f"ps.{b_mu}"], [ck], f"{tag}_c{ob}")
                yield
                PO(lambda g, cq_=cq_: g.tensor_mul(sq[si][:], cq_[:], cq_[:]),
                   [ck], [f"sq.{si}"], f"{tag}_s{ob}")
                b_sq = next_bank("att")
                TE(lambda te, b_sq=b_sq: te.matmul(
                    pb[b_sq][0:2, 0:CHUNK], hmaskb, sq[si][:], start=True, stop=True),
                   [f"sq.{si}", "consts"], [f"ps.{b_sq}"], f"{tag}_ss{ob}")
                SC(lambda s, b_sq=b_sq, srs=srs: s.activation(
                    srs[0:2, :], pb[b_sq][0:2, 0:CHUNK], AF.Ln,
                    scale=1.0 / D_HEAD, bias=epsv[0:2, :]),
                   [f"ps.{b_sq}", "epsv"], [rsk], f"{tag}_ln{ob}")
                SC(lambda s, srs=srs: s.activation(srs[0:2, :], srs[0:2, :],
                                                   AF.Exp, scale=-0.5),
                   [rsk], [rsk], f"{tag}_ex{ob}")
                yield
                b_rs = next_bank("att")
                TE(lambda te, b_rs=b_rs, srs=srs: te.matmul(
                    pb[b_rs][:, 0:CHUNK], hmT[:], srs[0:2, :], start=True, stop=True),
                   [rsk, "hmT"], [f"ps.{b_rs}"], f"{tag}_rb{ob}")
                VE(lambda v, b_rs=b_rs, cq_=cq_, tq_=tq_: v.tensor_mul(
                    tq_[:], cq_[:], pb[b_rs][:, 0:CHUNK]),
                   [ck, f"ps.{b_rs}"], [tk], f"{tag}_t{ob}")
                yield
                PO(lambda g, svw=svw, tq_=tq_: g.tensor_scalar(
                    svw, tq_[:], gv, bv, ALU.mult, ALU.add),
                   [tk, "scl"], [f"{srckey}.{ob}"], f"{tag}_o{ob}")
                yield

        def emit_attention(xslot, xkey, tag, extra_gens=()):
            xv = xs[xslot]
            extra = list(extra_gens)
            pend = {"n": 0}

            def step_extras(n=1, flush=False):
                pend["n"] += n
                if pend["n"] < 4 and not flush:
                    return
                k, pend["n"] = pend["n"], 0
                for _ in range(k):
                    for g in list(extra):
                        try:
                            next(g)
                        except StopIteration:
                            extra.remove(g)
            for mat, dst, dkey in (("wq", qraw, "qraw"), ("wk", kraw, "kraw")):
                for obg in range(2):
                    sl, _ = load_slabs(mat, list(range(obg * 4, obg * 4 + 4)),
                                       f"{tag}{mat}g{obg}")
                    for j in range(4):
                        ob = obg * 4 + j
                        bk = next_bank("att")
                        def mm_q(te, sl=sl, j=j, bk=bk):
                            for kc in range(IC_D):
                                inst = te.matmul(pb[bk][:, 0:CHUNK],
                                                 slab_tile(sl, j, IC_D, kc),
                                                 u[:, kc * CHUNK:(kc + 1) * CHUNK],
                                                 start=(kc == 0), stop=(kc == IC_D - 1))
                            return inst
                        TE(mm_q, [f"wslot.{sl}"] + [f"u.{i}" for i in range(IC_D)],
                           [f"ps.{bk}"], f"{tag}_{mat}m{ob}")
                        ev_eng = SC if mat == "wq" else VE
                        ev_fn = (lambda s, bk=bk, dst=dst, ob=ob: s.copy(
                            dst[:, ob * CHUNK:(ob + 1) * CHUNK],
                            pb[bk][:, 0:CHUNK])) if mat == "wq" else (
                            lambda v, bk=bk, dst=dst, ob=ob: v.tensor_copy(
                                dst[:, ob * CHUNK:(ob + 1) * CHUNK],
                                pb[bk][:, 0:CHUNK]))
                        ev_eng(ev_fn, [f"ps.{bk}"], [f"{dkey}.{ob}"],
                               f"{tag}_{mat}e{ob}")
                        step_extras()

            def gen_vproj():
                for half in range(2):
                    sl, _ = load_slabs("wv", list(range(half * 4, half * 4 + 4)),
                                       f"{tag}wvg{half}")
                    wview = wsl[sl][:, 0:WSLOT_W].rearrange(
                        "p (s rest) -> p s rest", s=4)
                    for ts in range(N_TSUB):
                        bk = next_bank("att")
                        def mm_v(te, sl=sl, bk=bk, ts=ts, wview=wview):
                            for kc in range(IC_D):
                                inst = te.matmul(
                                    pb[bk][:, 0:CHUNK],
                                    u[:, kc * CHUNK + ts * TSUB:kc * CHUNK + (ts + 1) * TSUB],
                                    wview[:, :, kc * 128:(kc + 1) * 128],
                                    start=(kc == 0), stop=(kc == IC_D - 1))
                            return inst
                        TE(mm_v, [f"wslot.{sl}"] + [f"u.{i}" for i in range(IC_D)],
                           [f"ps.{bk}"], f"{tag}_vm{ts}_{half}")
                        SC(lambda s, bk=bk, ts=ts, half=half: s.copy(
                            vtm[:, ts * D_MODEL + half * CHUNK:
                                ts * D_MODEL + (half + 1) * CHUNK],
                            pb[bk][:, 0:CHUNK]),
                           [f"ps.{bk}"],
                           [f"vtm.{ts}.{p}" for p in range(half * 4, half * 4 + 4)],
                           f"{tag}_ve{ts}_{half}")
                        yield

            gens = [gen_qk_headln(qraw, "qraw", qgv, qbv, 0, f"{tag}_qln"),
                    gen_qk_headln(kraw, "kraw", kgv, kbv, 1, f"{tag}_kln"),
                    gen_vproj()]
            while gens:
                nxt = []
                for g in gens:
                    try:
                        next(g)
                        nxt.append(g)
                    except StopIteration:
                        pass
                gens = nxt
                step_extras()
            AW = [512, 384, 256, 128]
            AOFF = [0, 512, 896, 1152]
            b_ds = B_DS
            ds_claimed = False
            for pr in range(N_PAIRS):
                qc = qln[:, pr * CHUNK:(pr + 1) * CHUNK]
                kc = kln[:, pr * CHUNK:(pr + 1) * CHUNK]
                abuf = asbh[pr % 2]
                abk = f"asbh.{pr % 2}"
                b01 = (next_bank("att"), next_bank("att"))
                for h in range(2):
                    r0 = h * 64
                    for sb in range(4):
                        bk = b01[sb % 2]
                        TE(lambda te, kc=kc, qc=qc, r0=r0, sb=sb, bk=bk:
                           te.matmul(pb[bk][:, 0:AW[sb]],
                                     kc[r0:r0 + 64, sb * 128:(sb + 1) * 128],
                                     qc[r0:r0 + 64, sb * 128:CHUNK],
                                     start=True, stop=True),
                           [f"kraw.{pr}", f"qraw.{pr}"], [f"ps.{bk}"],
                           f"{tag}_A{pr}_{h}_{sb}")
                        VE(lambda v, bk=bk, sb=sb, abuf=abuf, h=h:
                           v.tensor_mul(
                               abuf[:, h * 1280 + AOFF[sb]:
                                    h * 1280 + AOFF[sb] + AW[sb]],
                               pb[bk][:, 0:AW[sb]], mext[:, 0:AW[sb]]),
                           [f"ps.{bk}", "mext"], [abk], f"{tag}_Ad{pr}_{h}_{sb}")
                bO = next_bank("att")
                def mm_o(te, pr=pr, qc=qc, bO=bO, abuf=abuf):
                    for h in range(2):
                        r0 = h * 64
                        hd = 2 * pr + h
                        for sb in range(4):
                            vsl = vtm[:, sb * D_MODEL + hd * 64:
                                      sb * D_MODEL + (hd + 1) * 64]
                            te.matmul(pb[bO][r0:r0 + 64, sb * 128:CHUNK],
                                      vsl,
                                      abuf[:, h * 1280 + AOFF[sb]:
                                           h * 1280 + AOFF[sb] + AW[sb]],
                                      start=(sb == 0), stop=False)
                        Sb = SbfE if h == 0 else SbfO
                        inst = te.matmul(pb[bO][r0:r0 + 64, 0:CHUNK],
                                         Sb[:, pr * 64:(pr + 1) * 64], qc,
                                         start=False, stop=True)
                    return inst
                TE(mm_o, [abk, "SbfE", "SbfO", f"qraw.{pr}"]
                   + [f"vtm.{sb}.{pr}" for sb in range(4)],
                   [f"ps.{bO}"], f"{tag}_o{pr}")
                SC(lambda s, bO=bO, pr=pr: s.copy(
                    ofm[:, pr * CHUNK:(pr + 1) * CHUNK], pb[bO][:, 0:CHUNK]),
                   [f"ps.{bO}"], [f"ofm.{pr}"], f"{tag}_oe{pr}")
                bT = next_bank("att")
                def mm_t(te, kc=kc, bT=bT):
                    for sb in range(4):
                        inst = te.transpose(
                            pb_bf[bT][:, sb * 128:(sb + 1) * 128],
                            kc[:, sb * 128:(sb + 1) * 128], identb)
                    return inst
                TE(mm_t, [f"kraw.{pr}", "consts"], [f"ps.{bT}"], f"{tag}_kt{pr}")
                SC(lambda s, bT=bT, pr=pr: s.copy(ktmw[pr % 2][:],
                                                  pb_bf[bT][:, 0:CHUNK]),
                   [f"ps.{bT}"], [f"ktmw.{pr % 2}"], f"{tag}_kte{pr}")
                def mm_ds(te, pr=pr, b_ds=b_ds):
                    for h in range(2):
                        r0 = h * 64
                        hd = 2 * pr + h
                        for sb in range(4):
                            vsl = vtm[:, sb * D_MODEL + hd * 64:
                                      sb * D_MODEL + (hd + 1) * 64]
                            inst = te.matmul(
                                pb[b_ds][r0:r0 + 64, pr * 64:(pr + 1) * 64],
                                ktmw[pr % 2][:, sb * 128 + r0:sb * 128 + r0 + 64],
                                vsl, start=(sb == 0), stop=(sb == 3))
                    return inst
                ds_writes = [f"ps.{b_ds}.{pr}"]
                if not ds_claimed:
                    ds_claimed = True
                    ds_writes.append(f"ps.{b_ds}")
                TE(mm_ds, [f"ktmw.{pr % 2}"]
                   + [f"vtm.{sb}.{pr}" for sb in range(4)],
                   ds_writes, f"{tag}_ds{pr}")
                step_extras(2)
            VE(lambda v, b_ds=b_ds: v.tensor_add(Sf[:], Sf[:], pb[b_ds][:, 0:512]),
               ["Sf", f"ps.{b_ds}"] + [f"ps.{b_ds}.{p}" for p in range(N_PAIRS)],
               ["Sf"], f"{tag}_Sacc")
            SC(lambda s: s.copy(SbfE[0:64, :], Sf[0:64, :]),
               ["Sf"], ["SbfE"], f"{tag}_Sbe")
            SC(lambda s: s.copy(SbfO[64:128, :], Sf[64:128, :]),
               ["Sf"], ["SbfO"], f"{tag}_Sbo")
            step_extras(flush=True)
            for obg in range(2):
                sl, _ = load_slabs("wo", list(range(obg * 4, obg * 4 + 4)),
                                   f"{tag}wog{obg}")
                for j in range(4):
                    od = obg * 4 + j
                    bk = next_bank("att")
                    def mm_wo(te, sl=sl, j=j, bk=bk):
                        for e in range(IC_D):
                            inst = te.matmul(pb[bk][:, 0:CHUNK],
                                             slab_tile(sl, j, IC_D, e),
                                             ofm[:, e * CHUNK:(e + 1) * CHUNK],
                                             start=(e == 0), stop=(e == IC_D - 1))
                        return inst
                    TE(mm_wo, [f"wslot.{sl}"] + [f"ofm.{i}" for i in range(IC_D)],
                       [f"ps.{bk}"], f"{tag}_wom{od}")
                    VE(lambda v, bk=bk, od=od: v.tensor_add(
                        xv[:, od * CHUNK:(od + 1) * CHUNK],
                        xv[:, od * CHUNK:(od + 1) * CHUNK], pb[bk][:, 0:CHUNK]),
                       [f"ps.{bk}", f"{xkey}.{od}"], [f"{xkey}.{od}"],
                       f"{tag}_xr{od}")
                    step_extras()
            while extra:
                step_extras(4, flush=True)

        xT_r = xT.rearrange("(ic p) t -> p ic t", p=128)
        outT_r = outT.rearrange("(ic p) t -> p ic t", p=128)

        def xload(c):
            slot = c % 2
            xkey = f"x.{slot}"
            DMA("sy", lambda g, c=c, slot=slot: g.dma_start(
                out=xs[slot][:].rearrange("p (ic t) -> p ic t", t=CHUNK),
                in_=xT_r[:, :, c * CHUNK:(c + 1) * CHUNK]),
                [], [xkey] + [f"{xkey}.{od}" for od in range(IC_D)],
                f"xl{slot}", f"xload{c}")

        def xstore(c):
            slot = c % 2
            xkey = f"x.{slot}"
            DMA("sy", lambda g, c=c, slot=slot: g.dma_start(
                out=outT_r[:, :, c * CHUNK:(c + 1) * CHUNK],
                in_=xs[slot][:].rearrange("p (ic t) -> p ic t", t=CHUNK)),
                [xkey] + [f"{xkey}.{od}" for od in range(IC_D)],
                [f"stout.{c}"], f"st{slot}", f"xstore{c}")

        full = set(phases) == {"f1", "att", "f2"}
        if not full:
            for c in range(n_chunks):
                slot, xkey = c % 2, f"x.{c % 2}"
                xload(c)
                if "f1" in phases:
                    drain(gen_layernorm(slot, xkey, uf, "uf", f"c{c}ln1"))
                    drain(gen_gate(0, uf, "uf", f"c{c}f1"))
                    emit_hw3(0, slot, xkey, f"c{c}f1")
                if "att" in phases:
                    drain(gen_layernorm(slot, xkey, u, "u", f"c{c}ln2"))
                    emit_attention(slot, xkey, f"c{c}at")
                if "f2" in phases:
                    drain(gen_layernorm(slot, xkey, uf, "uf", f"c{c}ln3"))
                    drain(gen_gate(1, uf, "uf", f"c{c}f2"))
                    emit_hw3(1, slot, xkey, f"c{c}f2")
                xstore(c)
        else:
            xload(0)
            drain(gen_layernorm(0, "x.0", uf, "uf", "c0ln1"))
            drain(gen_gate(0, uf, "uf", "c0f1"))
            emit_hw3(0, 0, "x.0", "c0f1")
            for c in range(n_chunks):
                slot, xkey = c % 2, f"x.{c % 2}"
                nslot, nxkey = (c + 1) % 2, f"x.{(c + 1) % 2}"
                drain(gen_layernorm(slot, xkey, u, "u", f"c{c}ln2"))
                extras = []
                if c + 1 < n_chunks:
                    xload(c + 1)
                    extras = [gen_layernorm(nslot, nxkey, uf, "uf",
                                            f"c{c + 1}ln1", use_pool=True),
                              gen_gate(0, uf, "uf", f"c{c + 1}f1")]
                emit_attention(slot, xkey, f"c{c}at", extra_gens=extras)
                drain(gen_layernorm(slot, xkey, uf, "uf", f"c{c}ln3"))
                drain(gen_gate(1, uf, "uf", f"c{c}f2"))
                emit_hw3(1, slot, xkey, f"c{c}f2")
                xstore(c)
                if c + 1 < n_chunks:
                    emit_hw3(0, nslot, nxkey, f"c{c + 1}f1")

        final_waits = []
        for sname in ("st0", "st1"):
            if P.sem_count.get(sname, 0):
                final_waits.append((sname, P.sem_count[sname]))

        block = E(nc.Block())

        def replay(engine_name):
            def fn(eng):
                for op in P.ops[engine_name]:
                    for s, v in op.waits:
                        eng.wait_ge(sems[s], v)
                    inst = op.thunk(eng)
                    if op.inc is not None:
                        inst.then_inc(sems[op.inc[0]], op.inc[1])
                if engine_name == "sy":
                    for s, v in final_waits:
                        eng.wait_ge(sems[s], v)
            return fn

        block.tensor(replay("te"))
        block.scalar(replay("sc"))
        block.vector(replay("ve"))
        block.gpsimd(replay("po"))
        block.sync(replay("sy"))

    return nc



def _slab_tile(wT):
    K, O = wT.shape
    return np.ascontiguousarray(
        wT.reshape(K // 128, 128, O // 128, 128).transpose(2, 1, 0, 3))


def _scales(s):
    n = s.shape[0] // 128
    return np.ascontiguousarray(s.reshape(n, 128).T)


def make_weight_inputs(ins):
    g = lambda k: np.asarray(ins[k], np.float32)
    wm = {
        "w1": _slab_tile(g("L0_ffn_w1_w").T),
        "w2": _slab_tile(g("L0_ffn_w2_w").T),
        "h1": _slab_tile(g("L0_ffn_h1_w").T),
        "h2": _slab_tile(g("L0_ffn_h2_w").T),
        "w3": _slab_tile(g("L0_ffn_w3_w").T),
        "wq": _slab_tile(g("L0_attn_Wq").T),
        "wk": _slab_tile(g("L0_attn_Wk").T),
        "wv": _slab_tile(g("L0_attn_Wv").T),
        "wo": _slab_tile(g("L0_attn_Wo").T),
        "s1": _scales(g("L0_ffn_w1_s")),
        "s2": _scales(g("L0_ffn_w2_s")),
        "sh1": _scales(g("L0_ffn_h1_s")),
        "sh2": _scales(g("L0_ffn_h2_s")),
        "s3": _scales(g("L0_ffn_w3_s")),
        "qg": np.ascontiguousarray(np.tile(g("L0_attn_q_gamma"), 2)[:, None]),
        "qb": np.ascontiguousarray(np.tile(g("L0_attn_q_beta"), 2)[:, None]),
        "kg": np.ascontiguousarray(np.tile(g("L0_attn_k_gamma"), 2)[:, None]),
        "kb": np.ascontiguousarray(np.tile(g("L0_attn_k_beta"), 2)[:, None]),
        "identc": np.eye(128, dtype=np.float32),
        "maskc": np.ascontiguousarray(np.tril(np.ones((128, 128), np.float32)).T),
        "mextc": np.ascontiguousarray(np.concatenate(
            [np.tril(np.ones((128, 128), np.float32)).T,
             np.ones((128, 384), np.float32)], axis=1)),
        "onesc": np.ones((128, 1), np.float32),
        "onesrow": np.ones((1, 128), np.float32),
        "hmask": np.ascontiguousarray(
            np.stack([np.arange(128) < 64, np.arange(128) >= 64], 1)
            .astype(np.float32)),
        "hmaskT": np.ascontiguousarray(
            np.stack([np.arange(128) < 64, np.arange(128) >= 64], 0)
            .astype(np.float32)),
    }
    return wm


def kernel(**inputs):
    wm = make_weight_inputs(inputs)
    x = np.asarray(inputs["L0_attn_x"], np.float32)
    nc = build_program()
    in_maps = []
    for b in range(N_CORES):
        m = dict(wm)
        m["xT"] = np.ascontiguousarray(x[b].T)
        in_maps.append(m)
    res = run_bass_kernel_spmd(nc, in_maps, core_ids=list(range(N_CORES)))
    out = np.empty((N_CORES, T_FULL, D_MODEL), np.float32)
    for b in range(N_CORES):
        out[b] = res.results[b]["outT"].T
    return out


# version 17
# speedup vs baseline: 1.0252x; 1.0236x over previous
import numpy as np
from contextlib import ExitStack

import concourse.bass as bass
import concourse.mybir as mybir
from concourse.bass_utils import run_bass_kernel_spmd

F32 = mybir.dt.float32
BF16 = mybir.dt.bfloat16
AF = mybir.ActivationFunctionType
ALU = mybir.AluOpType

D_MODEL = 1024
D_HID = 4096
T_FULL = 2048
N_CORES = 8
CHUNK = 512
N_CHUNKS = T_FULL // CHUNK
TSUB = 128
N_TSUB = CHUNK // TSUB
N_HEADS = 16
D_HEAD = 64
N_PAIRS = 8
EPS = 1e-5

IC_D = D_MODEL // 128
IC_H = D_HID // 128

N_WSLOTS = 4
WSLOT_W = 4096



class Op:
    __slots__ = ("thunk", "waits", "inc", "tag")

    def __init__(self, thunk, waits, inc, tag):
        self.thunk = thunk
        self.waits = waits
        self.inc = inc
        self.tag = tag


class Prog:

    ENGINES = ("te", "sc", "ve", "po", "sy")

    def __init__(self):
        self.ops = {e: [] for e in self.ENGINES}
        self.sem_count = {}
        self.eng_sem = {"te": "s_te", "sc": "s_sc", "ve": "s_ve",
                        "po": "s_po", "sy": "s_sy"}
        for s in self.eng_sem.values():
            self.sem_count[s] = 0
        self.buf = {}
        self.knowledge = {e: {} for e in self.ENGINES}

    def _b(self, key):
        if key not in self.buf:
            self.buf[key] = {"w": None, "r": []}
        return self.buf[key]

    def emit(self, eng, thunk, reads=(), writes=(), dma_sem=None, tag=""):
        waits = {}

        def dep(token):
            if token is None:
                return
            s, v = token
            if waits.get(s, 0) < v:
                waits[s] = v

        for k in reads:
            dep(self._b(k)["w"])
        for k in writes:
            b = self._b(k)
            dep(b["w"])
            for r in b["r"]:
                dep(r)
        kn = self.knowledge[eng]
        wl = []
        for s, v in waits.items():
            if kn.get(s, 0) >= v:
                continue
            wl.append((s, v))
            kn[s] = v
        if dma_sem is not None:
            self.sem_count.setdefault(dma_sem, 0)
            self.sem_count[dma_sem] += 16
            token = (dma_sem, self.sem_count[dma_sem])
            inc = (dma_sem, 16)
        else:
            s = self.eng_sem[eng]
            self.sem_count[s] += 1
            token = (s, self.sem_count[s])
            inc = (s, 1)
        for k in reads:
            self._b(k)["r"].append(token)
        for k in writes:
            b = self._b(k)
            b["w"] = token
            b["r"] = []
        self.ops[eng].append(Op(thunk, wl, inc, tag))
        return token

    def barrier_token(self, key):
        return self._b(key)["w"]

    def n_instructions(self):
        return sum(len(v) for v in self.ops.values())



def build_program(n_chunks=N_CHUNKS, phases=("f1", "att", "f2")):
    nc = bass.Bass("TRN2", target_bir_lowering=False)

    dt_in = {}

    def din(name, shape):
        dt_in[name] = nc.dram_tensor(name, shape, F32, kind="ExternalInput").ap()
        return dt_in[name]

    xT = din("xT", [D_MODEL, T_FULL])
    w1 = din("w1", [IC_H, 128, IC_D, 128])
    w2 = din("w2", [IC_H, 128, IC_D, 128])
    h1 = din("h1", [IC_H, 128, IC_H, 128])
    h2 = din("h2", [IC_H, 128, IC_H, 128])
    w3 = din("w3", [IC_D, 128, IC_H, 128])
    wq = din("wq", [IC_D, 128, IC_D, 128])
    wk = din("wk", [IC_D, 128, IC_D, 128])
    wv = din("wv", [IC_D, 128, IC_D, 128])
    wo = din("wo", [IC_D, 128, IC_D, 128])
    s1 = din("s1", [128, IC_H])
    s2 = din("s2", [128, IC_H])
    sh1 = din("sh1", [128, IC_H])
    sh2 = din("sh2", [128, IC_H])
    s3 = din("s3", [128, IC_D])
    qg = din("qg", [128, 1])
    qb = din("qb", [128, 1])
    kg = din("kg", [128, 1])
    kb = din("kb", [128, 1])
    identc = din("identc", [128, 128])
    maskc = din("maskc", [128, 128])
    mextc = din("mextc", [128, CHUNK])
    onesc = din("onesc", [128, 1])
    onesrow = din("onesrow", [1, 128])
    hmask = din("hmask", [128, 2])
    hmaskT = din("hmaskT", [2, 128])
    outT = nc.dram_tensor("outT", [D_MODEL, T_FULL], F32, kind="ExternalOutput").ap()

    scr = {}
    for nm, t in (("w1", w1), ("w2", w2), ("h1", h1), ("h2", h2), ("w3", w3),
                  ("wq", wq), ("wk", wk), ("wv", wv), ("wo", wo)):
        scr[nm] = nc.dram_tensor("scr_" + nm, list(t.tensor.shape), BF16,
                                 kind="Internal").ap()

    P = Prog()

    with ExitStack() as stack:
        E = stack.enter_context

        xs = [E(nc.sbuf_tensor(f"xs{i}", [128, IC_D * CHUNK], F32)) for i in range(2)]
        u = E(nc.sbuf_tensor("u", [128, IC_D * CHUNK], BF16))
        hA = E(nc.sbuf_tensor("hA", [128, IC_H * CHUNK], BF16))
        hB = E(nc.sbuf_tensor("hB", [128, IC_H * CHUNK], BF16))
        wsl = [E(nc.sbuf_tensor(f"wsl{i}", [128, WSLOT_W], BF16)) for i in range(N_WSLOTS)]
        sq = [E(nc.sbuf_tensor(f"sq{i}", [128, CHUNK], BF16)) for i in range(2)]
        cqs = [E(nc.sbuf_tensor(f"cq{i}", [128, CHUNK], BF16)) for i in range(2)]
        tqs = [E(nc.sbuf_tensor(f"tq{i}", [128, CHUNK], BF16)) for i in range(2)]
        srow_mu = [E(nc.sbuf_tensor(f"srmu{i}", [2, CHUNK], BF16)) for i in range(2)]
        srow_rs = [E(nc.sbuf_tensor(f"srrs{i}", [2, CHUNK], BF16)) for i in range(2)]
        Sf = E(nc.sbuf_tensor("Sf", [128, N_PAIRS * D_HEAD], F32))
        SbfE = E(nc.sbuf_tensor("SbfE", [128, N_PAIRS * D_HEAD], BF16))
        SbfO = E(nc.sbuf_tensor("SbfO", [128, N_PAIRS * D_HEAD], BF16))
        asbh = [E(nc.sbuf_tensor(f"asbh{i}", [128, 2 * 1280], BF16))
                for i in range(2)]
        ktmw = [E(nc.sbuf_tensor(f"ktmw{i}", [128, CHUNK], BF16)) for i in range(2)]
        strow = E(nc.sbuf_tensor("strow", [2, CHUNK], BF16))
        strow2 = strow
        consts = E(nc.sbuf_tensor("consts", [128, 128 + 128 + 1 + 2], BF16))
        mext = E(nc.sbuf_tensor("mext", [128, CHUNK], BF16))
        musb = E(nc.sbuf_tensor("musb", [128, CHUNK], BF16))
        rsb = E(nc.sbuf_tensor("rsb", [128, CHUNK], BF16))
        onesr = E(nc.sbuf_tensor("onesr", [1, 128], BF16))
        hmT = E(nc.sbuf_tensor("hmT", [2, 128], BF16))
        scl = E(nc.sbuf_tensor("scl", [128, 4 * IC_H + 2 * IC_D + 4], F32))
        epsv = E(nc.sbuf_tensor("epsv", [128, 1], F32))
        onesf = E(nc.sbuf_tensor("onesf", [128, 1], F32))
        cen = hB[:, 0:IC_D * CHUNK]
        xbf = cen
        OVL = E(nc.sbuf_tensor("OVL", [128, 4 * IC_D * CHUNK], BF16))
        qraw = OVL[:, 0:4096]
        kraw = OVL[:, 4096:8192]
        vtm = OVL[:, 8192:12288]
        ofm = OVL[:, 12288:16384]
        qln, kln = qraw, kraw
        uf = E(nc.sbuf_tensor("uf", [128, IC_D * CHUNK], BF16))
        hG = [hA, hB]

        identb = consts[:, 0:128]
        maskb = consts[:, 128:256]
        onescol = consts[:, 256:257]
        hmaskb = consts[:, 257:259]
        s1v = scl[:, 0:IC_H]
        s2v = scl[:, IC_H:2 * IC_H]
        sh1v = scl[:, 2 * IC_H:3 * IC_H]
        sh2v = scl[:, 3 * IC_H:4 * IC_H]
        s3v = scl[:, 4 * IC_H:4 * IC_H + IC_D]
        s3hv = scl[:, 4 * IC_H + IC_D:4 * IC_H + 2 * IC_D]
        qgv = scl[:, 4 * IC_H + 2 * IC_D + 0:4 * IC_H + 2 * IC_D + 1]
        qbv = scl[:, 4 * IC_H + 2 * IC_D + 1:4 * IC_H + 2 * IC_D + 2]
        kgv = scl[:, 4 * IC_H + 2 * IC_D + 2:4 * IC_H + 2 * IC_D + 3]
        kbv = scl[:, 4 * IC_H + 2 * IC_D + 3:4 * IC_H + 2 * IC_D + 4]

        pb = [E(nc.psum_tensor(f"pb{i}", [128, 512], F32)) for i in range(8)]
        pb_bf = [b[:].tensor.bitcast(BF16).ap() for b in pb]

        sem_names = (["s_te", "s_sc", "s_ve", "s_po", "s_sy"]
                     + [f"wsl{i}" for i in range(N_WSLOTS)]
                     + [f"wslc{i}" for i in range(N_WSLOTS)]
                     + [f"wst{i}" for i in range(N_WSLOTS)]
                     + ["xl0", "xl1", "st0", "st1", "cst"])
        sems = {n: E(nc.semaphore(n)) for n in sem_names}


        def TE(thunk, reads, writes, tag=""):
            return P.emit("te", thunk, reads, writes, tag=tag)

        def SC(thunk, reads, writes, tag=""):
            return P.emit("sc", thunk, reads, writes, tag=tag)

        def VE(thunk, reads, writes, tag=""):
            return P.emit("ve", thunk, reads, writes, tag=tag)

        def PO(thunk, reads, writes, tag=""):
            return P.emit("po", thunk, reads, writes, tag=tag)

        def DMA(q, thunk, reads, writes, sem, tag=""):
            return P.emit(q, thunk, reads, writes, dma_sem=sem, tag=tag)

        DMA("po", lambda g: g.dma_start(out=consts[:, 0:128], in_=identc),
            [], ["consts", "cstall"], "cst", "ident")
        DMA("po", lambda g: g.dma_start(out=mext[:], in_=mextc),
            [], ["mext", "cstall"], "cst", "mext")
        DMA("po", lambda g: g.dma_start(out=consts[:, 256:257], in_=onesc),
            [], ["consts", "cstall"], "cst", "ones")
        DMA("po", lambda g: g.dma_start(out=onesr[:], in_=onesrow),
            [], ["onesr", "cstall"], "cst", "onesrow")
        DMA("po", lambda g: g.dma_start(out=hmT[:], in_=hmaskT),
            [], ["hmT", "cstall"], "cst", "hmT")
        DMA("po", lambda g: g.dma_start(out=consts[:, 257:259], in_=hmask),
            [], ["consts", "cstall"], "cst", "hmask")
        DMA("po", lambda g: g.dma_start(out=scl[:, 0:IC_H], in_=s1),
            [], ["scl", "cstall"], "cst", "s1")
        DMA("po", lambda g: g.dma_start(out=scl[:, IC_H:2 * IC_H], in_=s2),
            [], ["scl", "cstall"], "cst", "s2")
        DMA("po", lambda g: g.dma_start(out=scl[:, 2 * IC_H:3 * IC_H], in_=sh1),
            [], ["scl", "cstall"], "cst", "sh1")
        DMA("po", lambda g: g.dma_start(out=scl[:, 3 * IC_H:4 * IC_H], in_=sh2),
            [], ["scl", "cstall"], "cst", "sh2")
        DMA("po", lambda g: g.dma_start(out=scl[:, 4 * IC_H:4 * IC_H + IC_D], in_=s3),
            [], ["scl", "cstall"], "cst", "s3")
        base = 4 * IC_H + 2 * IC_D
        for i, t in enumerate((qg, qb, kg, kb)):
            DMA("po", lambda g, i=i, t=t: g.dma_start(
                out=scl[:, base + i:base + i + 1], in_=t),
                [], ["scl", "cstall"], "cst", "qkgb")
        VE(lambda v: v.memset(epsv[:], EPS), [], ["epsv"], "eps")
        VE(lambda v: v.memset(onesf[:], 1.0), [], ["onesf"], "onesf")
        VE(lambda v: v.tensor_scalar_mul(s3hv, s3v, 0.5), ["scl"], ["scl"], "s3h")
        VE(lambda v: v.memset(Sf[:], 0.0), [], ["Sf"], "S0")
        VE(lambda v: v.memset(SbfE[:], 0.0), [], ["SbfE"], "SbfE0")
        VE(lambda v: v.memset(SbfO[:], 0.0), [], ["SbfO"], "SbfO0")

        ring_state = {"next": 0}
        fp32_src = {"w1": w1, "w2": w2, "h1": h1, "h2": h2, "w3": w3,
                    "wq": wq, "wk": wk, "wv": wv, "wo": wo}
        scr_built = set()

        def load_slabs(mat, obs, tag):
            n_kc = {"w1": IC_D, "w2": IC_D, "h1": IC_H, "h2": IC_H,
                    "w3": IC_H, "wq": IC_D, "wk": IC_D, "wv": IC_D,
                    "wo": IC_D}[mat]
            slot = ring_state["next"]
            ring_state["next"] = (slot + 1) % N_WSLOTS
            w_per_slab = n_kc * 128
            assert w_per_slab * len(obs) <= WSLOT_W
            gkey = f"scrg_{mat}.{obs[0]}"
            sbuf_view = wsl[slot][:, 0:w_per_slab * len(obs)].rearrange(
                "p (s ko) -> p s ko", s=len(obs))
            if (mat, obs[0]) not in scr_built:
                scr_built.add((mat, obs[0]))
                src_ap = fp32_src[mat]

                def thunk(g, src_ap=src_ap, obs=obs, sbuf_view=sbuf_view):
                    o0 = obs[0]
                    return g.dma_start(
                        out=sbuf_view,
                        in_=src_ap[o0:o0 + len(obs)].rearrange(
                            "s p k o -> p s (k o)"))
                DMA("po", thunk, [], [f"wslot.{slot}"], f"wslc{slot}",
                    f"cld_{tag}")
                sc_ap = scr[mat]

                def sthunk(g, sc_ap=sc_ap, obs=obs, sbuf_view=sbuf_view):
                    o0 = obs[0]
                    return g.dma_start(
                        out=sc_ap[o0:o0 + len(obs)].rearrange(
                            "s p k o -> p s (k o)"),
                        in_=sbuf_view)
                DMA("sy", sthunk, [f"wslot.{slot}"], [gkey],
                    f"wst{slot}", f"sst_{tag}")
            else:
                sc_ap = scr[mat]

                def thunk(g, sc_ap=sc_ap, obs=obs, sbuf_view=sbuf_view):
                    o0 = obs[0]
                    return g.dma_start(
                        out=sbuf_view,
                        in_=sc_ap[o0:o0 + len(obs)].rearrange(
                            "s p k o -> p s (k o)"))
                DMA("sy", thunk, [gkey], [f"wslot.{slot}"], f"wsl{slot}",
                    f"ld_{tag}")
            return slot, n_kc

        def slab_tile(slot, slab_i, n_kc, kc):
            off = (slab_i * n_kc + kc) * 128
            return wsl[slot][:, off:off + 128]

        B_DS = 4
        psum_rr = {"att": 0, "ffn": 0}
        POOLS = {"att": (0, 1, 2, 3), "ffn": (5, 6, 7)}

        def next_bank(pool="ffn", exclude=()):
            ring = POOLS[pool]
            for _ in range(len(ring)):
                i = ring[psum_rr[pool] % len(ring)]
                psum_rr[pool] += 1
                if i not in exclude:
                    return i
            raise RuntimeError

        def ovl_keys(ob):
            if ob < 8:
                return [f"qraw.{ob}"]
            if ob < 16:
                return [f"kraw.{ob - 8}"]
            if ob < 24:
                c0 = (ob - 16) * 512
                p0 = (c0 % 1024) // 128
                return [f"vtm.{c0 // 1024}.{p}" for p in range(p0, p0 + 4)]
            return [f"ofm.{ob - 24}"]
        ALL_OVL = [k for ob in range(IC_H) for k in ovl_keys(ob)]

        def hh_view(ob):
            return OVL[:, ob * CHUNK:(ob + 1) * CHUNK]


        def drain(g):
            for _ in g:
                pass

        def gen_layernorm(xslot, xkey, ubuf, ukp, tag, use_pool=False):
            xv = xs[xslot]
            b_st = next_bank("ffn")
            def mm_mu(te):
                for ic in range(IC_D):
                    inst = te.matmul(pb[b_st][0:1, 0:CHUNK], onesf[:],
                                     xv[:, ic * CHUNK:(ic + 1) * CHUNK],
                                     start=(ic == 0), stop=(ic == IC_D - 1))
                return inst
            TE(mm_mu, [f"{xkey}.{ic}" for ic in range(IC_D)] + ["onesf"],
               [f"ps.{b_st}"], f"{tag}_musum")
            yield
            SC(lambda s: s.activation(strow[0:1, :], pb[b_st][0:1, 0:CHUNK],
                                      AF.Copy, scale=1.0 / D_MODEL),
               [f"ps.{b_st}"], ["strow"], f"{tag}_murow")
            b_mu = next_bank("ffn")
            TE(lambda te: te.matmul(pb[b_mu][:, 0:CHUNK], onesr[:], strow[0:1, :],
                                    start=True, stop=True),
               ["strow", "onesr"], [f"ps.{b_mu}"], f"{tag}_mubc")
            SC(lambda s: s.copy(musb[:], pb[b_mu][:, 0:CHUNK]),
               [f"ps.{b_mu}"], ["musb"], f"{tag}_musb")
            yield
            b_sq = next_bank("ffn")
            mub = musb[:].rearrange("p (a t) -> p a t", a=1).broadcast_to(
                [128, 4, CHUNK])
            for hf, eng in ((0, VE), (1, PO)):
                eng(lambda e, hf=hf: e.tensor_sub(
                    cen[:, hf * 4 * CHUNK:(hf + 1) * 4 * CHUNK].rearrange(
                        "p (i t) -> p i t", t=CHUNK),
                    xv[:, hf * 4 * CHUNK:(hf + 1) * 4 * CHUNK].rearrange(
                        "p (i t) -> p i t", t=CHUNK), mub),
                    [f"{xkey}.{ic}" for ic in range(hf * 4, hf * 4 + 4)]
                    + ["musb"],
                    [f"cen.h{hf}", "xbf"]
                    + [f"hg1.{ic}" for ic in range(hf * 4, hf * 4 + 4)],
                    f"{tag}_cen{hf}")
            for ic in range(IC_D):
                eng2 = PO if ic % 2 == 0 else VE
                eng2(lambda e, ic=ic: e.tensor_mul(
                    sq[ic % 2][:], cen[:, ic * CHUNK:(ic + 1) * CHUNK],
                    cen[:, ic * CHUNK:(ic + 1) * CHUNK]),
                    [f"cen.h{ic // 4}", "xbf"], [f"sq.{ic % 2}"], f"{tag}_sq{ic}")
                if ic % 2 == 1:
                    def mm_sq(te, ic=ic):
                        for j in (ic - 1, ic):
                            inst = te.matmul(pb[b_sq][0:1, 0:CHUNK], onescol,
                                             sq[j % 2][:],
                                             start=(j == 0), stop=(j == IC_D - 1))
                        return inst
                    TE(mm_sq, ["sq.0", "sq.1", "consts"],
                       [f"ps.{b_sq}"], f"{tag}_sqs{ic}")
            yield
            SC(lambda s: s.activation(strow2[0:1, :], pb[b_sq][0:1, 0:CHUNK],
                                      AF.Ln, scale=1.0 / D_MODEL, bias=epsv[0:1, :]),
               [f"ps.{b_sq}", "epsv"], ["strow"], f"{tag}_ln")
            SC(lambda s: s.activation(strow2[0:1, :], strow2[0:1, :],
                                      AF.Exp, scale=-0.5),
               ["strow"], ["strow"], f"{tag}_exp")
            b_rs = next_bank("ffn")
            TE(lambda te: te.matmul(pb[b_rs][:, 0:CHUNK], onesr[:],
                                    strow2[0:1, :], start=True, stop=True),
               ["strow", "onesr"], [f"ps.{b_rs}"], f"{tag}_rsbc")
            SC(lambda s: s.copy(rsb[:], pb[b_rs][:, 0:CHUNK]),
               [f"ps.{b_rs}"], ["rsb"], f"{tag}_rsb")
            yield
            rsbb = rsb[:].rearrange("p (a t) -> p a t", a=1).broadcast_to(
                [128, 4, CHUNK])
            for hf, eng in ((0, VE), (1, PO)):
                eng(lambda e, hf=hf, ubuf=ubuf: e.tensor_mul(
                    ubuf[:, hf * 4 * CHUNK:(hf + 1) * 4 * CHUNK].rearrange(
                        "p (i t) -> p i t", t=CHUNK),
                    cen[:, hf * 4 * CHUNK:(hf + 1) * 4 * CHUNK].rearrange(
                        "p (i t) -> p i t", t=CHUNK), rsbb),
                    [f"cen.h{hf}", "xbf", "rsb"],
                    [f"{ukp}.{ic}" for ic in range(hf * 4, hf * 4 + 4)],
                    f"{tag}_u{hf}")

        def gen_gate(gpar, ubuf, ukp, tag):
            G = hG[gpar]
            gk = f"hg{gpar}"
            for obg in range(IC_H // 4):
                sl1, _ = load_slabs("w1", list(range(obg * 4, obg * 4 + 4)), f"{tag}w1g{obg}")
                sl2, _ = load_slabs("w2", list(range(obg * 4, obg * 4 + 4)), f"{tag}w2g{obg}")
                for j in range(4):
                    ob = obg * 4 + j
                    ba = next_bank("ffn")
                    def mm_a(te, sl1=sl1, j=j, ba=ba, ubuf=ubuf):
                        for kc in range(IC_D):
                            inst = te.matmul(pb[ba][:, 0:CHUNK],
                                             slab_tile(sl1, j, IC_D, kc),
                                             ubuf[:, kc * CHUNK:(kc + 1) * CHUNK],
                                             start=(kc == 0), stop=(kc == IC_D - 1))
                        return inst
                    TE(mm_a, [f"wslot.{sl1}"] + [f"{ukp}.{i}" for i in range(IC_D)],
                       [f"ps.{ba}"], f"{tag}_a{ob}")
                    SC(lambda s, ba=ba, ob=ob, G=G: s.activation(
                        G[:, ob * CHUNK:(ob + 1) * CHUNK], pb[ba][:, 0:CHUNK],
                        AF.Silu, scale=s1v[:, ob:ob + 1]),
                       [f"ps.{ba}", "scl"], [f"{gk}.{ob}"], f"{tag}_silu{ob}")
                    bb = next_bank("ffn")
                    def mm_b(te, sl2=sl2, j=j, bb=bb, ubuf=ubuf):
                        for kc in range(IC_D):
                            inst = te.matmul(pb[bb][:, 0:CHUNK],
                                             slab_tile(sl2, j, IC_D, kc),
                                             ubuf[:, kc * CHUNK:(kc + 1) * CHUNK],
                                             start=(kc == 0), stop=(kc == IC_D - 1))
                        return inst
                    TE(mm_b, [f"wslot.{sl2}"] + [f"{ukp}.{i}" for i in range(IC_D)],
                       [f"ps.{bb}"], f"{tag}_b{ob}")
                    VE(lambda v, bb=bb, ob=ob, G=G: v.scalar_tensor_tensor(
                        G[:, ob * CHUNK:(ob + 1) * CHUNK], pb[bb][:, 0:CHUNK],
                        s2v[:, ob:ob + 1], G[:, ob * CHUNK:(ob + 1) * CHUNK],
                        ALU.mult, ALU.mult),
                       [f"ps.{bb}", "scl", f"{gk}.{ob}"], [f"{gk}.{ob}"],
                       f"{tag}_gate{ob}")
                    yield

        def emit_hw3(gpar, xslot, xkey, tag):
            xv = xs[xslot]
            G = hG[gpar]
            gk = f"hg{gpar}"
            for ob in range(IC_H):
                sl, _ = load_slabs("h1", [ob], f"{tag}h1s{ob}")
                bk = next_bank("ffn")
                def mm_h1(te, sl=sl, bk=bk, G=G):
                    for kc in range(IC_H):
                        inst = te.matmul(pb[bk][:, 0:CHUNK],
                                         slab_tile(sl, 0, IC_H, kc),
                                         G[:, kc * CHUNK:(kc + 1) * CHUNK],
                                         start=(kc == 0), stop=(kc == IC_H - 1))
                    return inst
                TE(mm_h1, [f"wslot.{sl}"] + [f"{gk}.{i}" for i in range(IC_H)],
                   [f"ps.{bk}"], f"{tag}_h1m{ob}")
                SC(lambda s, bk=bk, ob=ob: s.activation(
                    hh_view(ob), pb[bk][:, 0:CHUNK],
                    AF.Gelu, scale=sh1v[:, ob:ob + 1]),
                   [f"ps.{bk}", "scl"], ovl_keys(ob), f"{tag}_h1g{ob}")
            for ob in range(IC_H):
                sl, _ = load_slabs("h2", [ob], f"{tag}h2s{ob}")
                bk = next_bank("ffn")
                def mm_h2(te, sl=sl, bk=bk):
                    for kc in range(IC_H):
                        inst = te.matmul(pb[bk][:, 0:CHUNK],
                                         slab_tile(sl, 0, IC_H, kc),
                                         hh_view(kc),
                                         start=(kc == 0), stop=(kc == IC_H - 1))
                    return inst
                TE(mm_h2, [f"wslot.{sl}"] + ALL_OVL,
                   [f"ps.{bk}"], f"{tag}_h2m{ob}")
                SC(lambda s, bk=bk, ob=ob, G=G: s.activation(
                    G[:, ob * CHUNK:(ob + 1) * CHUNK], pb[bk][:, 0:CHUNK],
                    AF.Gelu, scale=sh2v[:, ob:ob + 1]),
                   [f"ps.{bk}", "scl"], [f"{gk}.{ob}"], f"{tag}_h2g{ob}")
            for ob in range(IC_D):
                sl, _ = load_slabs("w3", [ob], f"{tag}w3s{ob}")
                bk = next_bank("ffn")
                def mm_w3(te, sl=sl, bk=bk, G=G):
                    for kc in range(IC_H):
                        inst = te.matmul(pb[bk][:, 0:CHUNK],
                                         slab_tile(sl, 0, IC_H, kc),
                                         G[:, kc * CHUNK:(kc + 1) * CHUNK],
                                         start=(kc == 0), stop=(kc == IC_H - 1))
                    return inst
                TE(mm_w3, [f"wslot.{sl}"] + [f"{gk}.{i}" for i in range(IC_H)],
                   [f"ps.{bk}"], f"{tag}_w3m{ob}")
                VE(lambda v, bk=bk, ob=ob: v.scalar_tensor_tensor(
                    xv[:, ob * CHUNK:(ob + 1) * CHUNK], pb[bk][:, 0:CHUNK],
                    s3hv[:, ob:ob + 1], xv[:, ob * CHUNK:(ob + 1) * CHUNK],
                    ALU.mult, ALU.add),
                   [f"ps.{bk}", "scl", f"{xkey}.{ob}"], [f"{xkey}.{ob}"],
                   f"{tag}_res{ob}")

        def gen_qk_headln(src_t, srckey, gv, bv, si, tag):
            cq_, tq_ = cqs[si], tqs[si]
            smu, srs = srow_mu[si], srow_rs[si]
            ck, tk, muk, rsk = f"cq{si}", f"tq{si}", f"smu{si}", f"srs{si}"
            for ob in range(IC_D):
                svw = src_t[:, ob * CHUNK:(ob + 1) * CHUNK]
                b_st = next_bank("att")
                TE(lambda te, svw=svw, b_st=b_st: te.matmul(
                    pb[b_st][0:2, 0:CHUNK], hmaskb, svw, start=True, stop=True),
                   [f"{srckey}.{ob}", "consts"], [f"ps.{b_st}"], f"{tag}_mu{ob}")
                SC(lambda s, b_st=b_st, smu=smu: s.activation(
                    smu[:, :], pb[b_st][0:2, 0:CHUNK], AF.Copy, scale=1.0 / D_HEAD),
                   [f"ps.{b_st}"], [muk], f"{tag}_mur{ob}")
                b_mu = next_bank("att")
                TE(lambda te, b_mu=b_mu, smu=smu: te.matmul(
                    pb[b_mu][:, 0:CHUNK], hmT[:], smu[:, :], start=True, stop=True),
                   [muk, "hmT"], [f"ps.{b_mu}"], f"{tag}_mub{ob}")
                VE(lambda v, svw=svw, b_mu=b_mu, cq_=cq_: v.tensor_sub(
                    cq_[:], svw, pb[b_mu][:, 0:CHUNK]),
                   [f"{srckey}.{ob}", f"ps.{b_mu}"], [ck], f"{tag}_c{ob}")
                yield
                PO(lambda g, cq_=cq_: g.tensor_mul(sq[si][:], cq_[:], cq_[:]),
                   [ck], [f"sq.{si}"], f"{tag}_s{ob}")
                b_sq = next_bank("att")
                TE(lambda te, b_sq=b_sq: te.matmul(
                    pb[b_sq][0:2, 0:CHUNK], hmaskb, sq[si][:], start=True, stop=True),
                   [f"sq.{si}", "consts"], [f"ps.{b_sq}"], f"{tag}_ss{ob}")
                SC(lambda s, b_sq=b_sq, srs=srs: s.activation(
                    srs[0:2, :], pb[b_sq][0:2, 0:CHUNK], AF.Ln,
                    scale=1.0 / D_HEAD, bias=epsv[0:2, :]),
                   [f"ps.{b_sq}", "epsv"], [rsk], f"{tag}_ln{ob}")
                SC(lambda s, srs=srs: s.activation(srs[0:2, :], srs[0:2, :],
                                                   AF.Exp, scale=-0.5),
                   [rsk], [rsk], f"{tag}_ex{ob}")
                yield
                b_rs = next_bank("att")
                TE(lambda te, b_rs=b_rs, srs=srs: te.matmul(
                    pb[b_rs][:, 0:CHUNK], hmT[:], srs[0:2, :], start=True, stop=True),
                   [rsk, "hmT"], [f"ps.{b_rs}"], f"{tag}_rb{ob}")
                VE(lambda v, b_rs=b_rs, cq_=cq_, tq_=tq_: v.tensor_mul(
                    tq_[:], cq_[:], pb[b_rs][:, 0:CHUNK]),
                   [ck, f"ps.{b_rs}"], [tk], f"{tag}_t{ob}")
                yield
                PO(lambda g, svw=svw, tq_=tq_: g.tensor_scalar(
                    svw, tq_[:], gv, bv, ALU.mult, ALU.add),
                   [tk, "scl"], [f"{srckey}.{ob}"], f"{tag}_o{ob}")
                yield

        def emit_attention(xslot, xkey, tag, extra_gens=()):
            xv = xs[xslot]
            extra = list(extra_gens)
            pend = {"n": 0}

            def step_extras(n=1, flush=False):
                pend["n"] += n
                if pend["n"] < 4 and not flush:
                    return
                k, pend["n"] = pend["n"], 0
                for _ in range(k):
                    for g in list(extra):
                        try:
                            next(g)
                        except StopIteration:
                            extra.remove(g)
            for mat, dst, dkey in (("wq", qraw, "qraw"), ("wk", kraw, "kraw")):
                for obg in range(2):
                    sl, _ = load_slabs(mat, list(range(obg * 4, obg * 4 + 4)),
                                       f"{tag}{mat}g{obg}")
                    for j in range(4):
                        ob = obg * 4 + j
                        bk = next_bank("att")
                        def mm_q(te, sl=sl, j=j, bk=bk):
                            for kc in range(IC_D):
                                inst = te.matmul(pb[bk][:, 0:CHUNK],
                                                 slab_tile(sl, j, IC_D, kc),
                                                 u[:, kc * CHUNK:(kc + 1) * CHUNK],
                                                 start=(kc == 0), stop=(kc == IC_D - 1))
                            return inst
                        TE(mm_q, [f"wslot.{sl}"] + [f"u.{i}" for i in range(IC_D)],
                           [f"ps.{bk}"], f"{tag}_{mat}m{ob}")
                        ev_eng = SC if mat == "wq" else VE
                        ev_fn = (lambda s, bk=bk, dst=dst, ob=ob: s.copy(
                            dst[:, ob * CHUNK:(ob + 1) * CHUNK],
                            pb[bk][:, 0:CHUNK])) if mat == "wq" else (
                            lambda v, bk=bk, dst=dst, ob=ob: v.tensor_copy(
                                dst[:, ob * CHUNK:(ob + 1) * CHUNK],
                                pb[bk][:, 0:CHUNK]))
                        ev_eng(ev_fn, [f"ps.{bk}"], [f"{dkey}.{ob}"],
                               f"{tag}_{mat}e{ob}")
                        step_extras()

            def gen_vproj():
                for half in range(2):
                    sl, _ = load_slabs("wv", list(range(half * 4, half * 4 + 4)),
                                       f"{tag}wvg{half}")
                    wview = wsl[sl][:, 0:WSLOT_W].rearrange(
                        "p (s rest) -> p s rest", s=4)
                    for ts in range(N_TSUB):
                        bk = next_bank("att")
                        def mm_v(te, sl=sl, bk=bk, ts=ts, wview=wview):
                            for kc in range(IC_D):
                                inst = te.matmul(
                                    pb[bk][:, 0:CHUNK],
                                    u[:, kc * CHUNK + ts * TSUB:kc * CHUNK + (ts + 1) * TSUB],
                                    wview[:, :, kc * 128:(kc + 1) * 128],
                                    start=(kc == 0), stop=(kc == IC_D - 1))
                            return inst
                        TE(mm_v, [f"wslot.{sl}"] + [f"u.{i}" for i in range(IC_D)],
                           [f"ps.{bk}"], f"{tag}_vm{ts}_{half}")
                        SC(lambda s, bk=bk, ts=ts, half=half: s.copy(
                            vtm[:, ts * D_MODEL + half * CHUNK:
                                ts * D_MODEL + (half + 1) * CHUNK],
                            pb[bk][:, 0:CHUNK]),
                           [f"ps.{bk}"],
                           [f"vtm.{ts}.{p}" for p in range(half * 4, half * 4 + 4)],
                           f"{tag}_ve{ts}_{half}")
                        yield

            AW = [512, 384, 256, 128]
            AOFF = [0, 512, 896, 1152]
            b_ds = B_DS

            def gen_inner():
                ds_claimed = False
                for pr in range(N_PAIRS):
                    for _ in range(5 if pr == 0 else 4):
                        yield
                    qc = qln[:, pr * CHUNK:(pr + 1) * CHUNK]
                    kc = kln[:, pr * CHUNK:(pr + 1) * CHUNK]
                    abuf = asbh[pr % 2]
                    abk = f"asbh.{pr % 2}"
                    b01 = (next_bank("att"), next_bank("att"))
                    for h in range(2):
                        r0 = h * 64
                        for sb in range(4):
                            bk = b01[sb % 2]
                            TE(lambda te, kc=kc, qc=qc, r0=r0, sb=sb, bk=bk:
                               te.matmul(pb[bk][:, 0:AW[sb]],
                                         kc[r0:r0 + 64, sb * 128:(sb + 1) * 128],
                                         qc[r0:r0 + 64, sb * 128:CHUNK],
                                         start=True, stop=True),
                               [f"kraw.{pr}", f"qraw.{pr}"], [f"ps.{bk}"],
                               f"{tag}_A{pr}_{h}_{sb}")
                            VE(lambda v, bk=bk, sb=sb, abuf=abuf, h=h:
                               v.tensor_mul(
                                   abuf[:, h * 1280 + AOFF[sb]:
                                        h * 1280 + AOFF[sb] + AW[sb]],
                                   pb[bk][:, 0:AW[sb]], mext[:, 0:AW[sb]]),
                               [f"ps.{bk}", "mext"], [abk], f"{tag}_Ad{pr}_{h}_{sb}")
                    bO = next_bank("att")
                    def mm_o(te, pr=pr, qc=qc, bO=bO, abuf=abuf):
                        for h in range(2):
                            r0 = h * 64
                            hd = 2 * pr + h
                            for sb in range(4):
                                vsl = vtm[:, sb * D_MODEL + hd * 64:
                                          sb * D_MODEL + (hd + 1) * 64]
                                te.matmul(pb[bO][r0:r0 + 64, sb * 128:CHUNK],
                                          vsl,
                                          abuf[:, h * 1280 + AOFF[sb]:
                                               h * 1280 + AOFF[sb] + AW[sb]],
                                          start=(sb == 0), stop=False)
                            Sb = SbfE if h == 0 else SbfO
                            inst = te.matmul(pb[bO][r0:r0 + 64, 0:CHUNK],
                                             Sb[:, pr * 64:(pr + 1) * 64], qc,
                                             start=False, stop=True)
                        return inst
                    TE(mm_o, [abk, "SbfE", "SbfO", f"qraw.{pr}"]
                       + [f"vtm.{sb}.{pr}" for sb in range(4)],
                       [f"ps.{bO}"], f"{tag}_o{pr}")
                    SC(lambda s, bO=bO, pr=pr: s.copy(
                        ofm[:, pr * CHUNK:(pr + 1) * CHUNK], pb[bO][:, 0:CHUNK]),
                       [f"ps.{bO}"], [f"ofm.{pr}"], f"{tag}_oe{pr}")
                    bT = next_bank("att")
                    def mm_t(te, kc=kc, bT=bT):
                        for sb in range(4):
                            inst = te.transpose(
                                pb_bf[bT][:, sb * 128:(sb + 1) * 128],
                                kc[:, sb * 128:(sb + 1) * 128], identb)
                        return inst
                    TE(mm_t, [f"kraw.{pr}", "consts"], [f"ps.{bT}"], f"{tag}_kt{pr}")
                    SC(lambda s, bT=bT, pr=pr: s.copy(ktmw[pr % 2][:],
                                                      pb_bf[bT][:, 0:CHUNK]),
                       [f"ps.{bT}"], [f"ktmw.{pr % 2}"], f"{tag}_kte{pr}")
                    def mm_ds(te, pr=pr, b_ds=b_ds):
                        for h in range(2):
                            r0 = h * 64
                            hd = 2 * pr + h
                            for sb in range(4):
                                vsl = vtm[:, sb * D_MODEL + hd * 64:
                                          sb * D_MODEL + (hd + 1) * 64]
                                inst = te.matmul(
                                    pb[b_ds][r0:r0 + 64, pr * 64:(pr + 1) * 64],
                                    ktmw[pr % 2][:, sb * 128 + r0:sb * 128 + r0 + 64],
                                    vsl, start=(sb == 0), stop=(sb == 3))
                        return inst
                    ds_writes = [f"ps.{b_ds}.{pr}"]
                    if not ds_claimed:
                        ds_claimed = True
                        ds_writes.append(f"ps.{b_ds}")
                    TE(mm_ds, [f"ktmw.{pr % 2}"]
                       + [f"vtm.{sb}.{pr}" for sb in range(4)],
                       ds_writes, f"{tag}_ds{pr}")
                    step_extras(2)
                yield
            gens2 = [gen_qk_headln(qraw, "qraw", qgv, qbv, 0, f"{tag}_qln"),
                     gen_qk_headln(kraw, "kraw", kgv, kbv, 1, f"{tag}_kln"),
                     gen_vproj(), gen_inner()]
            while gens2:
                nxt2 = []
                for g in gens2:
                    try:
                        next(g)
                        nxt2.append(g)
                    except StopIteration:
                        pass
                gens2 = nxt2
                step_extras()
            VE(lambda v, b_ds=b_ds: v.tensor_add(Sf[:], Sf[:], pb[b_ds][:, 0:512]),
               ["Sf", f"ps.{b_ds}"] + [f"ps.{b_ds}.{p}" for p in range(N_PAIRS)],
               ["Sf"], f"{tag}_Sacc")
            SC(lambda s: s.copy(SbfE[0:64, :], Sf[0:64, :]),
               ["Sf"], ["SbfE"], f"{tag}_Sbe")
            SC(lambda s: s.copy(SbfO[64:128, :], Sf[64:128, :]),
               ["Sf"], ["SbfO"], f"{tag}_Sbo")
            step_extras(flush=True)
            for obg in range(2):
                sl, _ = load_slabs("wo", list(range(obg * 4, obg * 4 + 4)),
                                   f"{tag}wog{obg}")
                for j in range(4):
                    od = obg * 4 + j
                    bk = next_bank("att")
                    def mm_wo(te, sl=sl, j=j, bk=bk):
                        for e in range(IC_D):
                            inst = te.matmul(pb[bk][:, 0:CHUNK],
                                             slab_tile(sl, j, IC_D, e),
                                             ofm[:, e * CHUNK:(e + 1) * CHUNK],
                                             start=(e == 0), stop=(e == IC_D - 1))
                        return inst
                    TE(mm_wo, [f"wslot.{sl}"] + [f"ofm.{i}" for i in range(IC_D)],
                       [f"ps.{bk}"], f"{tag}_wom{od}")
                    VE(lambda v, bk=bk, od=od: v.tensor_add(
                        xv[:, od * CHUNK:(od + 1) * CHUNK],
                        xv[:, od * CHUNK:(od + 1) * CHUNK], pb[bk][:, 0:CHUNK]),
                       [f"ps.{bk}", f"{xkey}.{od}"], [f"{xkey}.{od}"],
                       f"{tag}_xr{od}")
                    step_extras()
            while extra:
                step_extras(4, flush=True)

        xT_r = xT.rearrange("(ic p) t -> p ic t", p=128)
        outT_r = outT.rearrange("(ic p) t -> p ic t", p=128)

        def xload(c):
            slot = c % 2
            xkey = f"x.{slot}"
            DMA("sy", lambda g, c=c, slot=slot: g.dma_start(
                out=xs[slot][:].rearrange("p (ic t) -> p ic t", t=CHUNK),
                in_=xT_r[:, :, c * CHUNK:(c + 1) * CHUNK]),
                [], [xkey] + [f"{xkey}.{od}" for od in range(IC_D)],
                f"xl{slot}", f"xload{c}")

        def xstore(c):
            slot = c % 2
            xkey = f"x.{slot}"
            DMA("sy", lambda g, c=c, slot=slot: g.dma_start(
                out=outT_r[:, :, c * CHUNK:(c + 1) * CHUNK],
                in_=xs[slot][:].rearrange("p (ic t) -> p ic t", t=CHUNK)),
                [xkey] + [f"{xkey}.{od}" for od in range(IC_D)],
                [f"stout.{c}"], f"st{slot}", f"xstore{c}")

        full = set(phases) == {"f1", "att", "f2"}
        if not full:
            for c in range(n_chunks):
                slot, xkey = c % 2, f"x.{c % 2}"
                xload(c)
                if "f1" in phases:
                    drain(gen_layernorm(slot, xkey, uf, "uf", f"c{c}ln1"))
                    drain(gen_gate(0, uf, "uf", f"c{c}f1"))
                    emit_hw3(0, slot, xkey, f"c{c}f1")
                if "att" in phases:
                    drain(gen_layernorm(slot, xkey, u, "u", f"c{c}ln2"))
                    emit_attention(slot, xkey, f"c{c}at")
                if "f2" in phases:
                    drain(gen_layernorm(slot, xkey, uf, "uf", f"c{c}ln3"))
                    drain(gen_gate(1, uf, "uf", f"c{c}f2"))
                    emit_hw3(1, slot, xkey, f"c{c}f2")
                xstore(c)
        else:
            xload(0)
            drain(gen_layernorm(0, "x.0", uf, "uf", "c0ln1"))
            drain(gen_gate(0, uf, "uf", "c0f1"))
            emit_hw3(0, 0, "x.0", "c0f1")
            for c in range(n_chunks):
                slot, xkey = c % 2, f"x.{c % 2}"
                nslot, nxkey = (c + 1) % 2, f"x.{(c + 1) % 2}"
                drain(gen_layernorm(slot, xkey, u, "u", f"c{c}ln2"))
                extras = []
                if c + 1 < n_chunks:
                    xload(c + 1)
                    extras = [gen_layernorm(nslot, nxkey, uf, "uf",
                                            f"c{c + 1}ln1", use_pool=True),
                              gen_gate(0, uf, "uf", f"c{c + 1}f1")]
                emit_attention(slot, xkey, f"c{c}at", extra_gens=extras)
                drain(gen_layernorm(slot, xkey, uf, "uf", f"c{c}ln3"))
                drain(gen_gate(1, uf, "uf", f"c{c}f2"))
                emit_hw3(1, slot, xkey, f"c{c}f2")
                xstore(c)
                if c + 1 < n_chunks:
                    emit_hw3(0, nslot, nxkey, f"c{c + 1}f1")

        final_waits = []
        for sname in ("st0", "st1"):
            if P.sem_count.get(sname, 0):
                final_waits.append((sname, P.sem_count[sname]))

        block = E(nc.Block())

        def replay(engine_name):
            def fn(eng):
                for op in P.ops[engine_name]:
                    for s, v in op.waits:
                        eng.wait_ge(sems[s], v)
                    inst = op.thunk(eng)
                    if op.inc is not None:
                        inst.then_inc(sems[op.inc[0]], op.inc[1])
                if engine_name == "sy":
                    for s, v in final_waits:
                        eng.wait_ge(sems[s], v)
            return fn

        block.tensor(replay("te"))
        block.scalar(replay("sc"))
        block.vector(replay("ve"))
        block.gpsimd(replay("po"))
        block.sync(replay("sy"))

    return nc



def _slab_tile(wT):
    K, O = wT.shape
    return np.ascontiguousarray(
        wT.reshape(K // 128, 128, O // 128, 128).transpose(2, 1, 0, 3))


def _scales(s):
    n = s.shape[0] // 128
    return np.ascontiguousarray(s.reshape(n, 128).T)


def make_weight_inputs(ins):
    g = lambda k: np.asarray(ins[k], np.float32)
    wm = {
        "w1": _slab_tile(g("L0_ffn_w1_w").T),
        "w2": _slab_tile(g("L0_ffn_w2_w").T),
        "h1": _slab_tile(g("L0_ffn_h1_w").T),
        "h2": _slab_tile(g("L0_ffn_h2_w").T),
        "w3": _slab_tile(g("L0_ffn_w3_w").T),
        "wq": _slab_tile(g("L0_attn_Wq").T),
        "wk": _slab_tile(g("L0_attn_Wk").T),
        "wv": _slab_tile(g("L0_attn_Wv").T),
        "wo": _slab_tile(g("L0_attn_Wo").T),
        "s1": _scales(g("L0_ffn_w1_s")),
        "s2": _scales(g("L0_ffn_w2_s")),
        "sh1": _scales(g("L0_ffn_h1_s")),
        "sh2": _scales(g("L0_ffn_h2_s")),
        "s3": _scales(g("L0_ffn_w3_s")),
        "qg": np.ascontiguousarray(np.tile(g("L0_attn_q_gamma"), 2)[:, None]),
        "qb": np.ascontiguousarray(np.tile(g("L0_attn_q_beta"), 2)[:, None]),
        "kg": np.ascontiguousarray(np.tile(g("L0_attn_k_gamma"), 2)[:, None]),
        "kb": np.ascontiguousarray(np.tile(g("L0_attn_k_beta"), 2)[:, None]),
        "identc": np.eye(128, dtype=np.float32),
        "maskc": np.ascontiguousarray(np.tril(np.ones((128, 128), np.float32)).T),
        "mextc": np.ascontiguousarray(np.concatenate(
            [np.tril(np.ones((128, 128), np.float32)).T,
             np.ones((128, 384), np.float32)], axis=1)),
        "onesc": np.ones((128, 1), np.float32),
        "onesrow": np.ones((1, 128), np.float32),
        "hmask": np.ascontiguousarray(
            np.stack([np.arange(128) < 64, np.arange(128) >= 64], 1)
            .astype(np.float32)),
        "hmaskT": np.ascontiguousarray(
            np.stack([np.arange(128) < 64, np.arange(128) >= 64], 0)
            .astype(np.float32)),
    }
    return wm


def kernel(**inputs):
    wm = make_weight_inputs(inputs)
    x = np.asarray(inputs["L0_attn_x"], np.float32)
    nc = build_program()
    in_maps = []
    for b in range(N_CORES):
        m = dict(wm)
        m["xT"] = np.ascontiguousarray(x[b].T)
        in_maps.append(m)
    res = run_bass_kernel_spmd(nc, in_maps, core_ids=list(range(N_CORES)))
    out = np.empty((N_CORES, T_FULL, D_MODEL), np.float32)
    for b in range(N_CORES):
        out[b] = res.results[b]["outT"].T
    return out
